# version 20
# baseline (speedup 1.0000x reference)
import numpy as np
from contextlib import ExitStack
import concourse.bass as bass
import concourse.mybir as mybir
from concourse.bass_utils import run_bass_kernel_spmd

F32 = mybir.dt.float32
BF16 = mybir.dt.bfloat16
AF = mybir.ActivationFunctionType
ALU = mybir.AluOpType
AX = mybir.AxisListType.X

CFG = dict(NP=8, NO=8, NFC=4, NE=8)
D = 1024
DT = 8
CH = 512

ENG_SEM = {"tensor": "sTE", "vector": "sDVE", "scalar": "sACT", "gpsimd": "sGPC",
           "sync": "sSYN"}


class Prog:

    def __init__(self, nc):
        self.nc = nc
        self.semh = {}
        self.cnt = {}
        self.tags = {}
        self.ops = []
        self.maxw = {}

    def add_sem(self, stack, name):
        self.semh[name] = stack.enter_context(self.nc.semaphore(name))
        self.cnt[name] = 0

    def emit(self, eng, build, waits=(), tag=None, inc=None):
        wl = []
        for w in waits:
            if w is None:
                continue
            wl.append(w if isinstance(w, tuple) else self.tags[w])
        if inc is None:
            inc = (ENG_SEM[eng], 1)
        self.ops.append((eng, wl, build, inc))
        self.cnt[inc[0]] += inc[1]
        if tag is not None:
            self.tags[tag] = (inc[0], self.cnt[inc[0]])

    def flush(self, block):
        by_eng = {}
        for op in self.ops:
            by_eng.setdefault(op[0], []).append(op)
        for engname, engops in by_eng.items():
            mw = self.maxw.setdefault(engname, {})

            def body(e, engops=engops, mw=mw):
                for (_, wl, build, inc) in engops:
                    for (s, v) in wl:
                        if v > mw.get(s, 0):
                            e.wait_ge(self.semh[s], v)
                            mw[s] = v
                    inst = build(e)
                    if inst is not None:
                        inst.then_inc(self.semh[inc[0]], inc[1])
            getattr(block, engname)(body)
        self.ops = []


def mm_chain(psum_ap, stat_list, mov_list):
    def b(e, psum_ap=psum_ap, stat_list=list(stat_list), mov_list=list(mov_list)):
        last = None
        n = len(stat_list)
        for i in range(n):
            last = e.matmul(psum_ap, stat_list[i], mov_list[i],
                            start=(i == 0), stop=(i == n - 1))
        return last
    return b


def build_nc(cfg=None):
    cfg = dict(CFG if cfg is None else cfg)
    NP, NO, NFC, NE = cfg["NP"], cfg["NO"], cfg["NFC"], cfg["NE"]
    NCH = NP + NO
    TOWN = NO * CH
    TPRE = NP * CH
    DFF = NFC * 1024
    HALF_T = TOWN // 2
    NG = HALF_T // CH
    NTTH = HALF_T // 128
    NTT = TOWN // 128
    assert NO % 2 == 0 and NG >= 1 and NTT % 4 == 0

    nc = bass.Bass("TRN2", target_bir_lowering=False)

    def dr(n_, s, dt=F32, kind="ExternalInput"):
        return nc.dram_tensor(n_, s, dt, kind=kind).ap()

    xprev = dr("xprev", [TPRE, D])
    xown = dr("xown", [TOWN, D])
    wrT_d = dr("wrT", [D, D])
    wiT_d = dr("wiT", [D, D])
    lam8 = dr("lam8", [128, DT])
    maskv = dr("maskv", [128, NCH])
    wrout_d = dr("wrout", [D, NE])
    br_d = dr("br", [NE, 1])
    w1 = dr("w1", [NE, D, DFF])
    b1_d = dr("b1r", [128, NE * NFC * 8])
    w2 = dr("w2", [NE, DFF, D])
    b2_d = dr("b2", [NE, D])
    ident_d = dr("ident", [128, 128])
    ecm1_d = dr("ecm1", [128, NE])
    out_d = dr("out", [TOWN, D], F32, kind="ExternalOutput")
    acc_d = dr("acc_d", [NTT, 128, D], F32, kind="Internal")
    SLOTS = 2 * TOWN
    C_CAP = SLOTS // NE
    NQ = SLOTS // 2048
    EPQ = NE // NQ
    TPE = C_CAP // 128
    NGE = C_CAP // 512
    NFC2 = DFF // 512
    assert C_CAP % 512 == 0 and EPQ * C_CAP == 2048 and NGE >= 1
    xn_tok_d = nc.dram_tensor("xn_tok_d", [TOWN, D], BF16, kind="Internal").ap()
    xsort_d = nc.dram_tensor("xsort_d", [SLOTS, D], BF16, kind="Internal").ap()
    ysort_d = nc.dram_tensor("ysort_d", [SLOTS, D], F32, kind="Internal").ap()

    with ExitStack() as outer:
        P = Prog(nc)
        for s in ["semx0", "semx1", "semxn", "semacc0", "semacc1", "semY",
                  "semz", "semt0", "semt1", "semsc", "semxs0", "semxs1",
                  "semgt", "semys", "semout", "semc", "semw0", "semw1",
                  "semwi", "sTE", "sDVE", "sACT", "sGPC", "sSYN"]:
            P.add_sem(outer, s)

        def sb(n_, s, dt=F32):
            return outer.enter_context(nc.sbuf_tensor(n_, s, dt))

        def psum(n_, s):
            return outer.enter_context(nc.psum_tensor(n_, s, F32))

        ident = sb("ident_s", [128, 128])
        lam_sb = sb("lam_sb", [128, DT])
        mask_sb = sb("mask_sb", [128, NCH])
        scales = sb("scales", [128, DT, NCH])
        lsig8 = sb("lsig8", [128, DT])
        lgs_t1 = sb("lgs_t1", [128, DT])
        br_sb = sb("br_sb", [NE, 1])
        wrout_sb = sb("wrout_sb", [128, DT, NE])
        carry = sb("carry", [128, DT])
        b1_sb = sb("b1_sb", [128, NE * NFC * 8])
        b2_sb = sb("b2_sb", [NE, D])
        b2pk = sb("b2pk", [NE, 2 * D], BF16)
        s_all = sb("s_all", [128, NTT, NE])
        lg_t = sb("lg_t", [128, NTT, NE])
        sTpk = sb("sTpk", [NE, 2 * NTT * 128], BF16)
        rstat = sb("rstat", [128, NTT, 4])
        constv = sb("constv", [128, 8])
        dest_i = sb("dest_i", [128, NTT], mybir.dt.int32)
        ex_sc = sb("ex_sc", [128, NE])
        xbnc = sb("xbnc", [128, 2, 1024], BF16)

        p512a = psum("p512a", [128, 512])
        p512b = psum("p512b", [128, 512])
        p512c = psum("p512c", [128, 512])
        p512d = psum("p512d", [128, 512])
        p1024a = psum("p1024a", [128, 1024])
        p1024b = psum("p1024b", [128, 1024])

        for j, v in enumerate([1e-5, -0.5, 2.0, 1e-38, 0.5]):
            P.emit("gpsimd", lambda e, j=j, v=v: e.memset(constv[:, j:j + 1], v))
        CONSTW = ("sGPC", P.cnt["sGPC"])

        for dst, src in [(ident[:], ident_d), (lam_sb[:], lam8),
                         (mask_sb[:], maskv), (br_sb[:], br_d),
                         (wrout_sb[:], wrout_d.rearrange("(dt p) e -> p dt e", p=128)),
                         (b1_sb[:], b1_d), (b2_sb[:], b2_d)]:
            P.emit("sync", lambda e, d=dst, s=src: e.dma_start(d, s), inc=("semc", 16))
        CW = ("semc", P.cnt["semc"])

        with ExitStack() as phA:
            def asb(n_, s, dt=F32):
                return phA.enter_context(nc.sbuf_tensor(n_, s, dt))
            x_sb = asb("x_sb", [128, 2, 4, 1024])
            xn_td = asb("xn_td", [128, 4, 1024])
            xnbf_c = asb("xnbf_c", [128, DT, CH], BF16)
            xnf32_c = asb("xnf32_c", [128, DT, CH])
            wrT_sb = asb("wrT_sb", [128, DT, D], BF16)
            wiT_sb = asb("wiT_sb", [128, DT, D], BF16)
            bA = asb("bA", [128, DT, CH])
            bT = asb("bT", [128, DT, CH])
            bU = asb("bU", [128, DT, CH])
            h_sb = asb("h_sb", [128, DT, CH])
            lgev = asb("lgev", [NE, CH])
            stat = asb("stat", [128, 2, 4, 8])
            msk_tok = asb("msk_tok", [128, 4, NE])
            pos_tok = asb("pos_tok", [128, 4, NE])
            dest_f4 = asb("dest_f4", [128, 4])
            dtmp = asb("dtmp", [128, NE])
            eCm1 = asb("eCm1", [128, NE])
            ones8 = asb("ones8", [NE, 512], BF16)
            cnt8 = asb("cnt8", [NE, 1])
            sterr2 = asb("sterr2", [NE, 512])
            mT_b = lgev
            posT_b = sterr2

            P.emit("gpsimd", lambda e: e.dma_start(
                wrT_sb[:], wrT_d.rearrange("(dt p) e -> p dt e", p=128)),
                inc=("semwi", 16))
            P.emit("gpsimd", lambda e: e.dma_start(
                wiT_sb[:], wiT_d.rearrange("(dt p) e -> p dt e", p=128)),
                inc=("semwi", 16))
            P.emit("gpsimd", lambda e: e.dma_start(b2pk[:, 0:D], b2_d), inc=("semwi", 16))
            WRW = B2W = ("semwi", P.cnt["semwi"])
            P.emit("gpsimd", lambda e: e.memset(xbnc[:, 0], 0.0), tag="zm")
            for j in range(SLOTS // 128):
                P.emit("gpsimd", lambda e, j=j: e.dma_start(
                    xsort_d[j * 128:(j + 1) * 128], xbnc[:, 0]),
                    waits=["zm"], inc=("semz", 16))
            ZFW = ("semz", P.cnt["semz"])
            P.emit("sync", lambda e: e.dma_start(eCm1[:], ecm1_d),
                   waits=[CW], inc=("semc", 16))
            ECW = ("semc", P.cnt["semc"])
            P.emit("vector", lambda e: e.memset(ones8[:], 1.0), tag="on8")

            P.emit("scalar", lambda e: e.activation(lgs_t1[:], lam_sb[:], AF.Abs),
                   waits=[CW], tag="ls_ab")
            P.emit("scalar", lambda e: e.activation(lgs_t1[:], lgs_t1[:], AF.Exp,
                                                    scale=-1.0),
                   waits=["ls_ab"], tag="ls_ex")
            P.emit("scalar", lambda e: e.activation(lgs_t1[:], lgs_t1[:], AF.Ln,
                                                    bias=1.0),
                   waits=["ls_ex"], tag="ls_ln")
            P.emit("scalar", lambda e: e.activation(lsig8[:], lam_sb[:], AF.Relu,
                                                    scale=-1.0),
                   waits=["ls_ln"], tag="ls_rl")
            P.emit("vector", lambda e: e.tensor_add(lsig8[:], lsig8[:], lgs_t1[:]),
                   waits=["ls_rl"], tag="ls_sp")
            P.emit("vector", lambda e: e.tensor_scalar_mul(lsig8[:], lsig8[:], -8.0),
                   waits=["ls_sp"], tag="ls_8")
            for dt_ in range(DT):
                P.emit("vector", lambda e, dt_=dt_: e.tensor_scalar(
                    scales[:, dt_, :], mask_sb[:], lsig8[:, dt_:dt_ + 1], None,
                    ALU.mult, ALU.bypass),
                    waits=["ls_8"], tag=f"scl_{dt_}")

            def emit_head(c):
                own = c >= NP
                cb = c % 2
                srct = xown if own else xprev
                coff = (c - NP) * CH if own else c * CH
                xw = []
                if c >= 2:
                    xw += [f"xntd_{c-2}_3", f"sq_{c-2}_3"]
                    if c - 2 >= NP:
                        xw += [f"acc_{c-2}_3"]
                sx = "semx0" if cb == 0 else "semx1"
                xw += [(sx, P.cnt[sx]),
                       ("semacc0", P.cnt["semacc0"]),
                       ("semacc1", P.cnt["semacc1"])]
                P.emit("sync", lambda e, cb=cb, srct=srct, coff=coff: e.dma_start(
                    x_sb[:, cb], srct[coff:coff + CH]
                    .rearrange("(t q) d -> q t d", q=128)),
                    waits=xw, inc=(sx, 16), tag=f"xl_{c}")
                XW = (sx, P.cnt[sx])
                for t in range(4):
                    sw = [XW]
                    if c >= 2:
                        sw += [f"negmr_{c-2}_{t}"]
                    P.emit("vector", lambda e, cb=cb, t=t: e.reduce_sum(
                        stat[:, cb, t, 0:1], x_sb[:, cb, t, :], axis=AX),
                        waits=sw, tag=f"s1_{c}_{t}")
                for t in range(4):
                    sw = [XW, f"sq_{c}_{t-1}" if t else (f"sq_{c-1}_3" if c else None)]
                    if c >= 2:
                        sw += [f"negmr_{c-2}_{t}"]
                    if c >= 1:
                        sw += [f"xnT_{c-1}_7"]
                    if c - 1 >= NP:
                        sw += [("semxn", P.cnt["semxn"])]
                    P.emit("scalar", lambda e, cb=cb, t=t: e.activation(
                        xn_td[:, t, :], x_sb[:, cb, t, :], AF.Square,
                        accum_out=stat[:, cb, t, 1:2]),
                        waits=sw, tag=f"sq_{c}_{t}")
                for t in range(4):
                    P.emit("vector", lambda e, cb=cb, t=t: e.tensor_scalar(
                        stat[:, cb, t, 2:3], stat[:, cb, t, 0:1], 1.0 / D, None,
                        ALU.mult, ALU.bypass),
                        waits=[f"s1_{c}_{t}"], tag=f"m_{c}_{t}")
                    P.emit("vector", lambda e, cb=cb, t=t: e.tensor_mul(
                        stat[:, cb, t, 3:4], stat[:, cb, t, 2:3], stat[:, cb, t, 2:3]),
                        waits=[f"m_{c}_{t}"], tag=f"m2_{c}_{t}")
                    P.emit("vector", lambda e, cb=cb, t=t: e.scalar_tensor_tensor(
                        stat[:, cb, t, 4:5], stat[:, cb, t, 1:2], 1.0 / D,
                        stat[:, cb, t, 3:4], ALU.mult, ALU.subtract),
                        waits=[f"sq_{c}_{t}", f"m2_{c}_{t}"], tag=f"var_{c}_{t}")
                for t in range(4):
                    P.emit("scalar", lambda e, cb=cb, t=t: e.activation(
                        stat[:, cb, t, 5:6], stat[:, cb, t, 4:5], AF.Ln,
                        bias=constv[:, 0:1]),
                        waits=[f"var_{c}_{t}", CONSTW], tag=f"lnv_{c}_{t}")
                    P.emit("scalar", lambda e, cb=cb, t=t: e.activation(
                        stat[:, cb, t, 6:7], stat[:, cb, t, 5:6], AF.Exp,
                        scale=constv[:, 1:2]),
                        waits=[f"lnv_{c}_{t}"], tag=f"rstd_{c}_{t}")
                for t in range(4):
                    P.emit("vector", lambda e, cb=cb, t=t: e.scalar_tensor_tensor(
                        stat[:, cb, t, 7:8], stat[:, cb, t, 2:3], -1.0,
                        stat[:, cb, t, 6:7], ALU.mult, ALU.mult),
                        waits=[f"rstd_{c}_{t}"], tag=f"negmr_{c}_{t}")
                for t in range(4):
                    P.emit("vector", lambda e, cb=cb, t=t: e.tensor_scalar(
                        xn_td[:, t, :], x_sb[:, cb, t, :], stat[:, cb, t, 6:7],
                        stat[:, cb, t, 7:8], ALU.mult, ALU.add),
                        waits=[f"negmr_{c}_{t}", f"sq_{c}_{t}"], tag=f"xntd_{c}_{t}")
                for dt_ in range(DT):
                    pT = p512c if dt_ % 2 == 0 else p512d
                    tw = [f"xntd_{c}_3", CW]
                    if dt_ >= 2:
                        tw += [f"xnf_{c}_{dt_-2}", f"xnb_{c}_{dt_-2}"]
                    elif c >= 1:
                        tw += [f"sgi_{c-1}_{6 + dt_}"]

                    def bxnT(e, pT=pT, dt_=dt_):
                        last = None
                        for t in range(4):
                            last = e.transpose(pT[:, t * 128:(t + 1) * 128],
                                               xn_td[:, t, dt_ * 128:(dt_ + 1) * 128],
                                               ident[:])
                        return last
                    P.emit("tensor", bxnT, waits=tw, tag=f"xnT_{c}_{dt_}")
                    bw = [f"xnT_{c}_{dt_}"]
                    if c >= 1:
                        bw += [f"ich_{c-1}_7"]
                    P.emit("vector", lambda e, pT=pT, dt_=dt_: e.tensor_copy(
                        xnbf_c[:, dt_, :], pT[:]), waits=bw, tag=f"xnb_{c}_{dt_}")
                    fw = [f"xnT_{c}_{dt_}"]
                    if c >= 1:
                        fw += [f"ixn_{c-1}_{dt_}"]
                        if c - 1 >= NP:
                            fw += [f"rtch_{c-1}"]
                    P.emit("vector", lambda e, pT=pT, dt_=dt_: e.tensor_copy(
                        xnf32_c[:, dt_, :], pT[:]), waits=fw, tag=f"xnf_{c}_{dt_}")
                if own:
                    P.emit("gpsimd", lambda e, coff=coff: e.dma_start(
                        xn_tok_d[coff:coff + CH]
                        .rearrange("(t q) d -> q t d", q=128),
                        xn_td[:]),
                        waits=[f"xntd_{c}_{t_}" for t_ in range(4)]
                        + [("semxn", P.cnt["semxn"])],
                        inc=("semxn", 16), tag=f"xnsp_{c}")

            def emit_mid(c):
                own = c >= NP
                for et in range(DT):
                    pr = p512a if et % 2 == 0 else p512b
                    rw = [f"xnb_{c}_7", WRW]
                    rn = c * 8 + et
                    if rn >= 2:
                        pc, pe = divmod(rn - 2, 8)
                        rw += [f"sgr_{pc}_{pe}"]
                    P.emit("tensor", mm_chain(
                        pr[:],
                        [wrT_sb[:, d_, et * 128:(et + 1) * 128] for d_ in range(DT)],
                        [xnbf_c[:, d_, :] for d_ in range(DT)]),
                        waits=rw, tag=f"rch_{c}_{et}")
                    sw = [f"rch_{c}_{et}"]
                    if c >= 1:
                        sw += [f"root_{c-1}_{et}"]
                    P.emit("scalar", lambda e, pr=pr, et=et: e.activation(
                        bT[:, et, :], pr[:], AF.Sigmoid),
                        waits=sw, tag=f"sgr_{c}_{et}")
                for et in range(DT):
                    pi = p512c if et % 2 == 0 else p512d
                    iw = [f"xnb_{c}_7"]
                    if et >= 2:
                        iw += [f"sgi_{c}_{et-2}"]
                    else:
                        iw += [f"xnf_{c}_{6 + et}"]
                    P.emit("tensor", mm_chain(
                        pi[:],
                        [wiT_sb[:, d_, et * 128:(et + 1) * 128] for d_ in range(DT)],
                        [xnbf_c[:, d_, :] for d_ in range(DT)]),
                        waits=iw, tag=f"ich_{c}_{et}")
                    sw = [f"ich_{c}_{et}"]
                    if c >= 1:
                        sw += [f"scan_{c-1}_{et}"]
                    P.emit("scalar", lambda e, pi=pi, et=et: e.activation(
                        bU[:, et, :], pi[:], AF.Sigmoid),
                        waits=sw, tag=f"sgi_{c}_{et}")
                for et in range(DT):
                    P.emit("vector", lambda e, et=et, c=c: e.tensor_scalar(
                        bT[:, et, :], bT[:, et, :], scales[:, et, c:c + 1], None,
                        ALU.mult, ALU.bypass),
                        waits=[f"sgr_{c}_{et}", f"scl_{et}"], tag=f"la_{c}_{et}")
                for et in range(DT):
                    P.emit("vector", lambda e, et=et: e.tensor_mul(
                        bU[:, et, :], bU[:, et, :], xnf32_c[:, et, :]),
                        waits=[f"sgi_{c}_{et}", f"xnf_{c}_{et}"], tag=f"ixn_{c}_{et}")
                if own:
                    rw = [f"xnf_{c}_7", CW]
                    if c - 1 >= NP:
                        rw += [f"lgcp_{c-1}"]
                    P.emit("tensor", mm_chain(
                        p1024b[0:NE, 0:512],
                        [wrout_sb[:, d_, :] for d_ in range(DT)],
                        [xnf32_c[:, d_, :] for d_ in range(DT)]),
                        waits=rw, tag=f"rtch_{c}")

            def emit_tail_sv(c):
                for et in range(DT):
                    aw = [f"la_{c}_{et}"]
                    if c >= 1:
                        aw += [f"scan_{c-1}_{et}"]
                    P.emit("scalar", lambda e, et=et: e.activation(
                        bA[:, et, :], bT[:, et, :], AF.Exp), waits=aw,
                        tag=f"a_{c}_{et}")
                for et in range(DT):
                    P.emit("scalar", lambda e, et=et: e.activation(
                        bT[:, et, :], bT[:, et, :], AF.Exp, scale=constv[:, 2:3]),
                        waits=[f"a_{c}_{et}", CONSTW], tag=f"e2_{c}_{et}")
                for et in range(DT):
                    P.emit("scalar", lambda e, et=et: e.activation(
                        bT[:, et, :], bT[:, et, :], AF.Relu, scale=-1.0, bias=1.0),
                        waits=[f"e2_{c}_{et}"], tag=f"qm_{c}_{et}")
                for et in range(DT):
                    rw2 = [f"qm_{c}_{et}"]
                    if c >= 1:
                        rw2 += [f"cc_{c-1}_{et}"]
                        if c - 1 >= NP:
                            rw2 += [f"hT_{c-1}_3"]
                    P.emit("scalar", lambda e, et=et: e.activation(
                        h_sb[:, et, :], bT[:, et, :], AF.Sqrt),
                        waits=rw2, tag=f"root_{c}_{et}")
                for et in range(DT):
                    P.emit("vector", lambda e, et=et, c=c: e.scalar_tensor_tensor(
                        bU[:, et, :], h_sb[:, et, :], mask_sb[:, c:c + 1],
                        bU[:, et, :], ALU.mult, ALU.mult),
                        waits=[f"root_{c}_{et}", f"ixn_{c}_{et}"], tag=f"b_{c}_{et}")
                for et in range(DT):
                    sw = [f"a_{c}_{et}", f"b_{c}_{et}"]
                    if c >= 1:
                        sw += [f"cc_{c-1}_{et}"]
                    init = 0.0 if c == 0 else carry[:, et:et + 1]
                    P.emit("vector", lambda e, et=et, init=init: e.tensor_tensor_scan(
                        h_sb[:, et, :], bA[:, et, :], bU[:, et, :], init,
                        ALU.mult, ALU.add), waits=sw, tag=f"scan_{c}_{et}")
                    P.emit("vector", lambda e, et=et: e.tensor_copy(
                        carry[:, et:et + 1], h_sb[:, et, CH - 1:CH]),
                        waits=[f"scan_{c}_{et}"], tag=f"cc_{c}_{et}")

            def emit_tail_pe(c):
                cb = c % 2
                XW = P.tags[f"xl_{c}"]
                for t in range(4):
                    tw = [f"scan_{c}_7", CW]
                    if t >= 1:
                        tw += [f"acc_{c}_{t-1}"]
                    elif c - 1 >= NP:
                        tw += [f"acc_{c-1}_3"]

                    def bhT(e, t=t):
                        last = None
                        for et in range(DT):
                            last = e.transpose(p1024a[:, et * 128:(et + 1) * 128],
                                               h_sb[:, et, t * 128:(t + 1) * 128],
                                               ident[:])
                        return last
                    P.emit("tensor", bhT, waits=tw, tag=f"hT_{c}_{t}")
                    sa = "semacc0" if t % 2 == 0 else "semacc1"
                    aw = [f"hT_{c}_{t}", XW]
                    P.emit("vector", lambda e, cb=cb, t=t: e.tensor_add(
                        x_sb[:, cb, t, :], x_sb[:, cb, t, :], p1024a[:]),
                        waits=aw, tag=f"acc_{c}_{t}")
                    P.emit("sync", lambda e, c=c, t=t, cb=cb: e.dma_start(
                        acc_d[(c - NP) * 4 + t], x_sb[:, cb, t, :]),
                        waits=[f"acc_{c}_{t}", (sa, P.cnt[sa])],
                        inc=(sa, 16))

            def emit_route(c):
                oc = c - NP
                for j in range(4):
                    tt = 4 * oc + j
                    P.emit("vector", lambda e, tt=tt: e.reduce_max(
                        rstat[:, tt, 0:1], lg_t[:, tt, :], axis=AX),
                        waits=[f"lgcp_{c}"], tag=f"mx_{tt}")
                    P.emit("vector", lambda e, tt=tt: e.tensor_scalar_mul(
                        rstat[:, tt, 1:2], rstat[:, tt, 0:1], -1.0),
                        waits=[f"mx_{tt}"], tag=f"nmx_{tt}")
                    P.emit("scalar", lambda e, tt=tt: e.activation(
                        ex_sc[:], lg_t[:, tt, :], AF.Exp, bias=rstat[:, tt, 1:2]),
                        waits=[f"nmx_{tt}"] + ([f"ex_{tt-1}"] if tt else []),
                        tag=f"ex_{tt}")
                    P.emit("vector", lambda e, tt=tt: e.reduce_sum(
                        rstat[:, tt, 2:3], ex_sc[:], axis=AX),
                        waits=[f"ex_{tt}"], tag=f"sm_{tt}")
                    P.emit("vector", lambda e, tt=tt: e.reciprocal(
                        rstat[:, tt, 3:4], rstat[:, tt, 2:3]),
                        waits=[f"sm_{tt}"], tag=f"gt_{tt}")
                    P.emit("vector", lambda e, tt=tt, j=j: e.tensor_scalar(
                        msk_tok[:, j, :], lg_t[:, tt, :], rstat[:, tt, 0:1],
                        None, ALU.is_equal, ALU.bypass),
                        waits=[f"gt_{tt}"], tag=f"msk_{tt}")
                    P.emit("vector", lambda e, tt=tt, j=j: e.tensor_scalar(
                        s_all[:, tt, :], msk_tok[:, j, :], rstat[:, tt, 3:4],
                        None, ALU.mult, ALU.bypass),
                        waits=[f"msk_{tt}"], tag=f"sall_{tt}")
                tw = [f"sall_{4 * oc + j}" for j in range(4)] + [CW]
                if oc >= 1:
                    tw += [f"poscp_{oc-1}"]

                def bsT(e, oc=oc):
                    last = None
                    for j in range(4):
                        last = e.transpose(
                            p1024b[0:NE, 512 + j * 128:512 + (j + 1) * 128],
                            s_all[:, 4 * oc + j, :], ident[:])
                    return last
                P.emit("tensor", bsT, waits=tw, tag=f"sT_{oc}")
                P.emit("vector", lambda e, oc=oc: e.tensor_copy(
                    sTpk[:, oc * 512:(oc + 1) * 512], p1024b[0:NE, 512:1024]),
                    waits=[f"sT_{oc}"], tag=f"sThc_{oc}")
                P.emit("vector", lambda e, oc=oc: e.tensor_tensor(
                    sterr2[:], p1024b[0:NE, 512:1024],
                    sTpk[:, oc * 512:(oc + 1) * 512], ALU.subtract),
                    waits=[f"sThc_{oc}"], tag=f"sTc_{oc}")
                P.emit("vector", lambda e, oc=oc: e.tensor_copy(
                    sTpk[:, NTT * 128 + oc * 512:NTT * 128 + (oc + 1) * 512],
                    sterr2[:]),
                    waits=[f"sTc_{oc}"], tag=f"sTlc_{oc}")
                P.emit("vector", lambda e: e.tensor_scalar(
                    mT_b[:, 0:512], p1024b[0:NE, 512:1024], 0.0, None,
                    ALU.is_gt, ALU.bypass),
                    waits=[f"sTlc_{oc}", f"lgtr_{c}"], tag=f"mT_{oc}")
                init = 0.0 if oc == 0 else cnt8[:, 0:1]
                P.emit("vector", lambda e, init=init: e.tensor_tensor_scan(
                    posT_b[:], ones8[:], mT_b[:, 0:512], init, ALU.mult, ALU.add),
                    waits=[f"mT_{oc}", "on8"]
                    + ([f"cnt8_{oc-1}"] if oc else []), tag=f"posT_{oc}")
                P.emit("vector", lambda e: e.tensor_copy(
                    cnt8[:], posT_b[:, 511:512]),
                    waits=[f"posT_{oc}"], tag=f"cnt8_{oc}")

                def bpT(e):
                    last = None
                    for j in range(4):
                        last = e.transpose(
                            p1024b[:, 512 + j * NE:512 + (j + 1) * NE],
                            posT_b[:, j * 128:(j + 1) * 128],
                            ident[0:NE, 0:NE])
                    return last
                P.emit("tensor", bpT, waits=[f"posT_{oc}", f"mT_{oc}"],
                       tag=f"posbT_{oc}")
                P.emit("vector", lambda e: e.tensor_copy(
                    pos_tok[:].rearrange("p t e -> p (t e)"),
                    p1024b[:, 512:512 + 4 * NE]),
                    waits=[f"posbT_{oc}"], tag=f"poscp_{oc}")
                for j in range(4):
                    tt = 4 * oc + j
                    P.emit("vector", lambda e, j=j: e.tensor_scalar(
                        dtmp[:], pos_tok[:, j, :], float(C_CAP), None,
                        ALU.min, ALU.bypass),
                        waits=[f"poscp_{oc}", f"dg_{tt-1}" if tt else None],
                        tag=f"da_{tt}")
                    P.emit("vector", lambda e: e.tensor_add(
                        dtmp[:], dtmp[:], eCm1[:]),
                        waits=[f"da_{tt}", ECW], tag=f"db_{tt}")
                    P.emit("vector", lambda e, j=j: e.tensor_mul(
                        dtmp[:], dtmp[:], msk_tok[:, j, :]),
                        waits=[f"db_{tt}"], tag=f"dc_{tt}")
                    P.emit("vector", lambda e, j=j: e.reduce_sum(
                        dest_f4[:, j:j + 1], dtmp[:], axis=AX),
                        waits=[f"dc_{tt}"], tag=f"dg_{tt}")
                P.emit("vector", lambda e, oc=oc: e.tensor_copy(
                    dest_i[:, oc * 4:(oc + 1) * 4], dest_f4[:]),
                    waits=[f"dg_{4 * oc + 3}"], tag=f"dsti_{oc}")
                xnspW = P.tags[f"xnsp_{c}"]
                for hb in range(2):
                    scprev = ("semsc", P.cnt["semsc"])
                    P.emit("sync", lambda e, oc=oc, hb=hb: e.dma_start(
                        xbnc[:], xn_tok_d[oc * 512 + hb * 256:
                                          oc * 512 + (hb + 1) * 256]
                        .rearrange("(t q2) d -> q2 t d", q2=128)),
                        waits=[xnspW, scprev, ("semt0", P.cnt["semt0"])]
                        + ([ZFW] if oc == 0 and hb == 0 else []),
                        inc=("semt0", 16))
                    XBW = ("semt0", P.cnt["semt0"])
                    for j2 in range(2):
                        tt = 4 * oc + hb * 2 + j2
                        P.emit("gpsimd", lambda e, tt=tt, j2=j2:
                               e.indirect_dma_start(
                                   xsort_d,
                                   bass.IndirectOffsetOnAxis(
                                       dest_i[:, tt:tt + 1], 0),
                                   xbnc[:, j2], None),
                               waits=[ZFW, XBW, f"dsti_{oc}"],
                               inc=("semsc", 16))

            def emit_lg(c):
                coff = (c - NP) * CH
                rtw = [f"rtch_{c}", CW]
                if c - 1 >= NP:
                    rtw += [f"lgtr_{c-1}", f"posT_{c-1-NP}"]
                P.emit("vector", lambda e: e.tensor_scalar(
                    lgev[:], p1024b[0:NE, 0:512], br_sb[:, 0:1], None,
                    ALU.add, ALU.bypass),
                    waits=rtw, tag=f"rt_{c}")

                def blgtr(e):
                    last = None
                    for t in range(4):
                        last = e.transpose(p1024b[:, t * NE:(t + 1) * NE],
                                           lgev[:, t * 128:(t + 1) * 128],
                                           ident[0:NE, 0:NE])
                    return last
                P.emit("tensor", blgtr, waits=[f"rt_{c}"], tag=f"lgtr_{c}")
                P.emit("vector", lambda e, coff=coff: e.tensor_copy(
                    lg_t[:, coff // 128:coff // 128 + 4, :]
                    .rearrange("p t e -> p (t e)"),
                    p1024b[:, 0:4 * NE]),
                    waits=[f"lgtr_{c}"], tag=f"lgcp_{c}")

            for c in range(NCH + 1):
                if c < NCH:
                    emit_head(c)
                if c >= 1:
                    emit_tail_sv(c - 1)
                    if c - 1 >= NP:
                        emit_tail_pe(c - 1)
                        emit_lg(c - 1)
                        emit_route(c - 1)
                if c < NCH:
                    emit_mid(c)

            P.tags["desti"] = P.tags[f"dsti_{NO - 1}"]
            ACCW0 = ("semacc0", P.cnt["semacc0"])
            ACCW1 = ("semacc1", P.cnt["semacc1"])
            XNW = ("semxn", P.cnt["semxn"])
            endA = [(s, P.cnt[s]) for s in
                    ["sTE", "sDVE", "sACT", "semxn", "semacc0", "semacc1",
                     "semx0", "semx1", "semwi"]]

            with nc.Block() as blkA:
                P.flush(blkA)

        with ExitStack() as phB:
            def bsb(n_, s, dt=F32):
                return phB.enter_context(nc.sbuf_tensor(n_, s, dt))
            sterr = bsb("sterr", [NE, 512])
            ygt = bsb("ygt", [128, 4, 1024])
            yacc = bsb("yacc", [128, 16, 1024])
            w1c = bsb("w1c", [128, 2, DT, 512], BF16)
            w2c = bsb("w2c", [128, 2, 4, 1024], BF16)
            h1m = bsb("h1m", [128, 2, 4, CH], BF16)
            xse = bsb("xse", [128, 2, DT, C_CAP], BF16)

            for j in range(2):
                P.emit("vector", lambda e, j=j: e.tensor_sub(
                    sterr[:], b2_sb[:, j * 512:(j + 1) * 512],
                    b2pk[:, j * 512:(j + 1) * 512]),
                    waits=endA + [CW, B2W] + ([f"b2l_{j-1}"] if j else []),
                    tag=f"b2e_{j}")
                P.emit("vector", lambda e, j=j: e.tensor_copy(
                    b2pk[:, D + j * 512:D + (j + 1) * 512], sterr[:]),
                    waits=[f"b2e_{j}"], tag=f"b2l_{j}")


            SCW = ("semsc", P.cnt["semsc"])

            fi_n = 0
            g_n = 0
            tt_n = 0
            ytag = []
            ylast = {}
            wdone = []
            yprev = {}
            hc_by_exp = []
            yspart = {}
            for q in range(NQ):
                for e2 in range(EPQ):
                    e_ = q * EPQ + e2
                    xpar = e_ % 2
                    sxs = "semxs0" if xpar == 0 else "semxs1"
                    xw2 = [SCW, (sxs, P.cnt[sxs])]
                    if e_ >= 2:
                        xw2 += [hc_by_exp[e_ - 2]]
                    else:
                        xw2 += endA
                    P.emit("sync", lambda e, e_=e_, xpar=xpar: e.dma_start_transpose(
                        xse[:, xpar], xsort_d[e_ * C_CAP:(e_ + 1) * C_CAP]),
                        waits=xw2, inc=(sxs, 16))
                    XSW = (sxs, P.cnt[sxs])
                    for fc in range(NFC2):
                        k = e_ * NFC2 + fc
                        kb = k % 2
                        swp = "semw0" if kb == 0 else "semw1"
                        ww = [wdone[k - 2]] if k >= 2 else list(endA)
                        ww += [(swp, P.cnt[swp])]
                        P.emit("gpsimd", lambda e, e_=e_, fc=fc, kb=kb: e.dma_start(
                            w1c[:, kb], w1[e_, :, fc * 512:(fc + 1) * 512]
                            .rearrange("(dt q2) f -> q2 dt f", q2=128)),
                            waits=ww, inc=(swp, 16))
                        P.emit("gpsimd", lambda e, e_=e_, fc=fc, kb=kb: e.dma_start(
                            w2c[:, kb], w2[e_, fc * 512:(fc + 1) * 512, :]
                            .rearrange("(ft q2) n -> q2 ft n", q2=128)),
                            inc=(swp, 16))
                        wlev = (swp, P.cnt[swp])

                        def emit_Hpair():
                            nonlocal fi_n
                            for fi in range(4):
                                pA, pB = ((p512a, p512b) if fi_n % 2 == 0
                                          else (p512c, p512d))
                                hw = [wlev, XSW]
                                if fi_n >= 2:
                                    hw += [f"heB_{fi_n - 2}"]

                                def bH(e, pA=pA, pB=pB, fi=fi, kb=kb, xpar=xpar):
                                    last = None
                                    for d_ in range(DT):
                                        w = w1c[:, kb, d_, fi * 128:(fi + 1) * 128]
                                        last = e.matmul(
                                            pA[:], w, xse[:, xpar, d_, 0:CH],
                                            start=(d_ == 0), stop=(d_ == DT - 1))
                                        last = e.matmul(
                                            pB[:], w, xse[:, xpar, d_, CH:2 * CH],
                                            start=(d_ == 0), stop=(d_ == DT - 1))
                                    return last
                                P.emit("tensor", bH, waits=hw, tag=f"hc_{fi_n}")
                                col = e_ * NFC * 8 + fc * 4 + fi
                                swA = [f"hc_{fi_n}"]
                                swB = [f"hc_{fi_n}"]
                                if 0 in ylast:
                                    swA += [ylast[0]]
                                if 1 in ylast:
                                    swB += [ylast[1]]
                                P.emit("scalar", lambda e, pA=pA, fi=fi, col=col:
                                       e.activation(h1m[:, 0, fi, :], pA[:],
                                                    AF.Relu,
                                                    bias=b1_sb[:, col:col + 1]),
                                       waits=swA, tag=f"heA_{fi_n}")
                                P.emit("scalar", lambda e, pB=pB, fi=fi, col=col:
                                       e.activation(h1m[:, 1, fi, :], pB[:],
                                                    AF.Relu,
                                                    bias=b1_sb[:, col:col + 1]),
                                       waits=swB, tag=f"heB_{fi_n}")
                                fi_n += 1

                        def emit_Y(g):
                            nonlocal tt_n
                            helast = (f"heA_{fi_n - 1}" if g == 0
                                      else f"heB_{fi_n - 1}")
                            lastyc = None
                            for tti in range(4):
                                tile = e2 * TPE + g * 4 + tti
                                py = p1024a if tt_n % 2 == 0 else p1024b
                                yw2 = [helast]
                                if tt_n >= 2:
                                    yw2 += [ytag[tt_n - 2]]

                                def bY(e, py=py, g=g, tti=tti, kb=kb):
                                    last = None
                                    for fi in range(4):
                                        for dh in range(2):
                                            last = e.matmul(
                                                py[:, dh * 512:(dh + 1) * 512],
                                                h1m[:, g, fi,
                                                    tti * 128:(tti + 1) * 128],
                                                w2c[:, kb, fi,
                                                    dh * 512:(dh + 1) * 512],
                                                start=(fi == 0), stop=(fi == 3))
                                    return last
                                P.emit("tensor", bY, waits=yw2, tag=f"yc_{tt_n}_1")
                                vw = [f"yc_{tt_n}_1"]
                                if fc == 0 and tile in yspart:
                                    vw += [yspart[tile]]
                                if tile in yprev:
                                    vw += [yprev[tile]]
                                if fc == 0:
                                    P.emit("vector", lambda e, py=py, tile=tile:
                                           e.tensor_copy(yacc[:, tile, :], py[:]),
                                           waits=vw, tag=f"ye_{tt_n}")
                                else:
                                    P.emit("vector", lambda e, py=py, tile=tile:
                                           e.tensor_add(yacc[:, tile, :],
                                                        yacc[:, tile, :], py[:]),
                                           waits=vw, tag=f"ye_{tt_n}")
                                ytag.append(f"ye_{tt_n}")
                                yprev[tile] = f"ye_{tt_n}"
                                lastyc = f"yc_{tt_n}_1"
                                tt_n += 1
                            ylast[g] = lastyc

                        assert NGE == 2
                        emit_Hpair()
                        emit_Y(0)
                        emit_Y(1)
                        wdone.append(f"yc_{tt_n - 1}_1")
                    hc_by_exp.append(f"hc_{fi_n - 1}")
                for part in range(4):
                    P.emit("scalar", lambda e, q=q, part=part: e.dma_start(
                        ysort_d[q * 2048 + part * 512:q * 2048 + (part + 1) * 512]
                        .rearrange("(t q2) d -> q2 t d", q2=128),
                        yacc[:, part * 4:(part + 1) * 4, :]),
                        waits=[yprev[part * 4 + j] for j in range(4)]
                        + [("semys", P.cnt["semys"])],
                        inc=("semys", 16))
                    for j in range(4):
                        yspart[part * 4 + j] = ("semys", P.cnt["semys"])
                yprev = {}
            YSF = ("semys", P.cnt["semys"])

            for p in range(2):
                ypart = {}
                for part in range(NTTH // 4):
                    yw = [ACCW0, ACCW1, YSF, ("semY", P.cnt["semY"])] + endA
                    if p == 1:
                        yw += [("semout", P.cnt["semout"])]
                    P.emit("sync", lambda e, p=p, part=part: e.dma_start(
                        yacc[:, part * 4:(part + 1) * 4, :],
                        acc_d[p * NTTH + part * 4:p * NTTH + (part + 1) * 4]
                        .rearrange("t q2 d -> q2 t d")),
                        waits=yw, inc=("semY", 16))
                    for j in range(4):
                        ypart[part * 4 + j] = ("semY", P.cnt["semY"])
                outparts = []
                for tt in range(NTTH):
                    ttg = p * NTTH + tt
                    par = tt % 4
                    gw = [YSF, ("semgt", P.cnt["semgt"]), "desti"]
                    if ttg >= 4:
                        gw += [f"yga_{ttg - 4}"]
                    P.emit("gpsimd", lambda e, ttg=ttg, par=par: e.indirect_dma_start(
                        ygt[:, par], None,
                        ysort_d, bass.IndirectOffsetOnAxis(
                            dest_i[:, ttg:ttg + 1], 0)),
                        waits=gw, inc=("semgt", 16))
                    GW = ("semgt", P.cnt["semgt"])
                    py = p1024a if tt_n % 2 == 0 else p1024b
                    for dh in range(2):
                        bw = [f"sTlc_{NTT // 4 - 1}", "b2l_1"]
                        if dh == 0 and tt_n >= 2:
                            bw += [ytag[tt_n - 2]]
                        P.emit("tensor", mm_chain(
                            py[:, dh * 512:(dh + 1) * 512],
                            [sTpk[:, ttg * 128:(ttg + 1) * 128],
                             sTpk[:, ttg * 128:(ttg + 1) * 128],
                             sTpk[:, NTT * 128 + ttg * 128:
                                  NTT * 128 + (ttg + 1) * 128],
                             sTpk[:, NTT * 128 + ttg * 128:
                                  NTT * 128 + (ttg + 1) * 128]],
                            [b2pk[:, dh * 512:(dh + 1) * 512],
                             b2pk[:, D + dh * 512:D + (dh + 1) * 512],
                             b2pk[:, dh * 512:(dh + 1) * 512],
                             b2pk[:, D + dh * 512:D + (dh + 1) * 512]]),
                            waits=bw, tag=f"b2c_{tt_n}_{dh}")
                    vw = [GW, ypart[tt], f"gt_{ttg}"]
                    P.emit("vector", lambda e, tt=tt, ttg=ttg, par=par:
                           e.scalar_tensor_tensor(
                               yacc[:, tt, :], ygt[:, par],
                               rstat[:, ttg, 3:4], yacc[:, tt, :],
                               ALU.mult, ALU.add),
                           waits=vw, tag=f"yga_{ttg}")
                    P.emit("vector", lambda e, py=py, tt=tt: e.tensor_add(
                        yacc[:, tt, :], yacc[:, tt, :], py[:]),
                        waits=[f"b2c_{tt_n}_0", f"b2c_{tt_n}_1", f"yga_{ttg}"],
                        tag=f"b2a_{tt_n}")
                    ytag.append(f"b2a_{tt_n}")
                    lastb2 = f"b2a_{tt_n}"
                    tt_n += 1
                    if tt % 4 == 3:
                        P.emit("scalar", lambda e, p=p, tt=tt: e.dma_start(
                            out_d[p * HALF_T + (tt - 3) * 128:
                                  p * HALF_T + (tt + 1) * 128].rearrange(
                                "(t q2) d -> q2 t d", q2=128),
                            yacc[:, tt - 3:tt + 1, :]),
                            waits=[f"b2a_{tt_n - 1 - j}" for j in range(4)]
                            + [("semout", P.cnt["semout"])],
                            inc=("semout", 16))

            P.emit("sync", lambda e: e.nop(),
                   waits=[("semout", P.cnt["semout"])])

            with nc.Block() as blkB:
                P.flush(blkB)

    return nc


def kernel(L0_attn_x, L0_attn_W_r, L0_attn_W_i, L0_attn_lambd,
           L0_ffn_Wr, L0_ffn_br, L0_ffn_W1, L0_ffn_b1, L0_ffn_W2, L0_ffn_b2):
    x = np.ascontiguousarray(np.asarray(L0_attn_x, dtype=np.float32))
    assert x.shape == (4, 8192, 1024)
    wrT = np.ascontiguousarray(np.asarray(L0_attn_W_r, np.float32).T)
    wiT = np.ascontiguousarray(np.asarray(L0_attn_W_i, np.float32).T)
    lam8 = np.ascontiguousarray(np.asarray(L0_attn_lambd, np.float32).reshape(8, 128).T)
    wrout = np.ascontiguousarray(np.asarray(L0_ffn_Wr, np.float32))
    br = np.ascontiguousarray(np.asarray(L0_ffn_br, np.float32).reshape(8, 1))
    w1 = np.ascontiguousarray(np.asarray(L0_ffn_W1, np.float32))
    b1r = np.ascontiguousarray(
        np.asarray(L0_ffn_b1, np.float32).reshape(8, 32, 128).transpose(2, 0, 1)
        .reshape(128, 256))
    w2 = np.ascontiguousarray(np.asarray(L0_ffn_W2, np.float32))
    b2 = np.ascontiguousarray(np.asarray(L0_ffn_b2, np.float32))
    ident = np.eye(128, dtype=np.float32)
    ecm1 = np.ascontiguousarray(np.broadcast_to(
        np.arange(8, dtype=np.float32) * 1024.0 - 1.0, (128, 8)).astype(np.float32))

    nc = build_nc()
    in_maps = []
    for core in range(8):
        b, hh = core // 2, core % 2
        xo = np.ascontiguousarray(x[b, hh * 4096:(hh + 1) * 4096])
        xp = np.ascontiguousarray(x[b, 0:4096]) if hh == 1 else xo
        maskv = np.ones((128, 16), dtype=np.float32)
        if hh == 0:
            maskv[:, 0:8] = 0.0
        in_maps.append(dict(
            xprev=xp, xown=xo, wrT=wrT, wiT=wiT, lam8=lam8, maskv=maskv,
            wrout=wrout, br=br, w1=w1, b1r=b1r, w2=w2, b2=b2, ident=ident,
            ecm1=ecm1))

    res = run_bass_kernel_spmd(nc, in_maps, core_ids=list(range(8)))
    out = np.empty((4, 8192, 1024), dtype=np.float32)
    for core in range(8):
        b, hh = core // 2, core % 2
        out[b, hh * 4096:(hh + 1) * 4096] = res.results[core]["out"]
    return out


# version 22
# speedup vs baseline: 1.1874x; 1.1874x over previous
import numpy as np
from contextlib import ExitStack
import concourse.bass as bass
import concourse.mybir as mybir
from concourse.bass_utils import run_bass_kernel_spmd

F32 = mybir.dt.float32
BF16 = mybir.dt.bfloat16
AF = mybir.ActivationFunctionType
ALU = mybir.AluOpType
AX = mybir.AxisListType.X

CFG = dict(NP=8, NO=8, NFC=4, NE=8)
D = 1024
DT = 8
CH = 512

ENG_SEM = {"tensor": "sTE", "vector": "sDVE", "scalar": "sACT", "gpsimd": "sGPC",
           "sync": "sSYN"}


class Prog:

    def __init__(self, nc):
        self.nc = nc
        self.semh = {}
        self.cnt = {}
        self.tags = {}
        self.ops = []
        self.maxw = {}

    def add_sem(self, stack, name):
        self.semh[name] = stack.enter_context(self.nc.semaphore(name))
        self.cnt[name] = 0

    def emit(self, eng, build, waits=(), tag=None, inc=None):
        wl = []
        for w in waits:
            if w is None:
                continue
            wl.append(w if isinstance(w, tuple) else self.tags[w])
        if inc is None:
            inc = (ENG_SEM[eng], 1)
        self.ops.append((eng, wl, build, inc))
        self.cnt[inc[0]] += inc[1]
        if tag is not None:
            self.tags[tag] = (inc[0], self.cnt[inc[0]])

    def flush(self, block):
        by_eng = {}
        for op in self.ops:
            by_eng.setdefault(op[0], []).append(op)
        for engname, engops in by_eng.items():
            mw = self.maxw.setdefault(engname, {})

            def body(e, engops=engops, mw=mw):
                for (_, wl, build, inc) in engops:
                    for (s, v) in wl:
                        if v > mw.get(s, 0):
                            e.wait_ge(self.semh[s], v)
                            mw[s] = v
                    inst = build(e)
                    if inst is not None:
                        inst.then_inc(self.semh[inc[0]], inc[1])
            getattr(block, engname)(body)
        self.ops = []


def mm_chain(psum_ap, stat_list, mov_list):
    def b(e, psum_ap=psum_ap, stat_list=list(stat_list), mov_list=list(mov_list)):
        last = None
        n = len(stat_list)
        for i in range(n):
            last = e.matmul(psum_ap, stat_list[i], mov_list[i],
                            start=(i == 0), stop=(i == n - 1))
        return last
    return b


def build_nc(cfg=None):
    cfg = dict(CFG if cfg is None else cfg)
    NP, NO, NFC, NE = cfg["NP"], cfg["NO"], cfg["NFC"], cfg["NE"]
    NCH = NP + NO
    TOWN = NO * CH
    TPRE = NP * CH
    DFF = NFC * 1024
    HALF_T = TOWN // 2
    NG = HALF_T // CH
    NTTH = HALF_T // 128
    NTT = TOWN // 128
    assert NO % 2 == 0 and NG >= 1 and NTT % 4 == 0

    nc = bass.Bass("TRN2", target_bir_lowering=False)

    def dr(n_, s, dt=F32, kind="ExternalInput"):
        return nc.dram_tensor(n_, s, dt, kind=kind).ap()

    xprev = dr("xprev", [TPRE, D])
    xown = dr("xown", [TOWN, D])
    wrT_d = dr("wrT", [D, D])
    wiT_d = dr("wiT", [D, D])
    lam8 = dr("lam8", [128, DT])
    maskv = dr("maskv", [128, NCH])
    wrout_d = dr("wrout", [D, NE])
    br_d = dr("br", [NE, 1])
    w1 = dr("w1", [NE, D, DFF])
    b1_d = dr("b1r", [128, NE * NFC * 8])
    w2 = dr("w2", [NE, DFF, D])
    b2_d = dr("b2", [NE, D])
    ident_d = dr("ident", [128, 128])
    ecm1_d = dr("ecm1", [128, NE])
    out_d = dr("out", [TOWN, D], F32, kind="ExternalOutput")
    acc_d = dr("acc_d", [NTT, 128, D], F32, kind="Internal")
    SLOTS = 2 * TOWN
    C_CAP = SLOTS // NE
    NQ = SLOTS // 2048
    EPQ = NE // NQ
    TPE = C_CAP // 128
    NGE = C_CAP // 512
    NFC2 = DFF // 512
    assert C_CAP % 512 == 0 and EPQ * C_CAP == 2048 and NGE >= 1
    xn_tok_d = nc.dram_tensor("xn_tok_d", [TOWN, D], BF16, kind="Internal").ap()
    xsort_d = nc.dram_tensor("xsort_d", [SLOTS, D], BF16, kind="Internal").ap()
    ysort_d = nc.dram_tensor("ysort_d", [SLOTS, D], F32, kind="Internal").ap()

    with ExitStack() as outer:
        P = Prog(nc)
        for s in ["semx0", "semx1", "semxn", "semacc0", "semacc1", "semY",
                  "semz", "semt0", "semt1", "semsc", "semxs0", "semxs1",
                  "semgt", "semys", "semout", "semc", "semw0", "semw1",
                  "semwi", "sTE", "sDVE", "sACT", "sGPC", "sSYN"]:
            P.add_sem(outer, s)

        def sb(n_, s, dt=F32):
            return outer.enter_context(nc.sbuf_tensor(n_, s, dt))

        def psum(n_, s):
            return outer.enter_context(nc.psum_tensor(n_, s, F32))

        ident = sb("ident_s", [128, 128])
        lam_sb = sb("lam_sb", [128, DT])
        mask_sb = sb("mask_sb", [128, NCH])
        scales = sb("scales", [128, DT, NCH])
        lsig8 = sb("lsig8", [128, DT])
        lgs_t1 = sb("lgs_t1", [128, DT])
        br_sb = sb("br_sb", [NE, 1])
        wrout_sb = sb("wrout_sb", [128, DT, NE])
        carry = sb("carry", [128, DT])
        b1_sb = sb("b1_sb", [128, NE * NFC * 8])
        b2_sb = sb("b2_sb", [NE, D])
        b2pk = sb("b2pk", [NE, 2 * D], BF16)
        s_all = sb("s_all", [128, NTT, NE])
        lg_t = sb("lg_t", [128, NTT, NE])
        sTpk = sb("sTpk", [NE, 2 * NTT * 128], BF16)
        rstat = sb("rstat", [128, NTT, 4])
        constv = sb("constv", [128, 8])
        dest_i = sb("dest_i", [128, NTT], mybir.dt.int32)
        ex_sc = sb("ex_sc", [128, NE])
        xbnc = sb("xbnc", [128, 2, 1024], BF16)

        p512a = psum("p512a", [128, 512])
        p512b = psum("p512b", [128, 512])
        p512c = psum("p512c", [128, 512])
        p512d = psum("p512d", [128, 512])
        p1024a = psum("p1024a", [128, 1024])
        p1024b = psum("p1024b", [128, 1024])

        for j, v in enumerate([1e-5, -0.5, 2.0, 1e-38, 0.5]):
            P.emit("gpsimd", lambda e, j=j, v=v: e.memset(constv[:, j:j + 1], v))
        CONSTW = ("sGPC", P.cnt["sGPC"])

        for dst, src in [(ident[:], ident_d), (lam_sb[:], lam8),
                         (mask_sb[:], maskv), (br_sb[:], br_d),
                         (wrout_sb[:], wrout_d.rearrange("(dt p) e -> p dt e", p=128)),
                         (b1_sb[:], b1_d), (b2_sb[:], b2_d)]:
            P.emit("sync", lambda e, d=dst, s=src: e.dma_start(d, s), inc=("semc", 16))
        CW = ("semc", P.cnt["semc"])

        with ExitStack() as phA:
            def asb(n_, s, dt=F32):
                return phA.enter_context(nc.sbuf_tensor(n_, s, dt))
            x_sb = asb("x_sb", [128, 2, 4, 1024])
            xn_td = asb("xn_td", [128, 4, 1024])
            xnbf_c = asb("xnbf_c", [128, DT, CH], BF16)
            xnf32_c = asb("xnf32_c", [128, DT, CH])
            wrT_sb = asb("wrT_sb", [128, DT, D], BF16)
            wiT_sb = asb("wiT_sb", [128, DT, D], BF16)
            bA = asb("bA", [128, DT, CH])
            bT = asb("bT", [128, DT, CH])
            bU = asb("bU", [128, DT, CH])
            h_sb = asb("h_sb", [128, DT, CH])
            lgev = asb("lgev", [NE, CH])
            stat = asb("stat", [128, 2, 4, 8])
            msk_tok = asb("msk_tok", [128, 4, NE])
            pos_tok = asb("pos_tok", [128, 4, NE])
            dest_f4 = asb("dest_f4", [128, 4])
            dtmp = asb("dtmp", [128, NE])
            eCm1 = asb("eCm1", [128, NE])
            ones8 = asb("ones8", [NE, 512], BF16)
            cnt8 = asb("cnt8", [NE, 1])
            sterr2 = asb("sterr2", [NE, 512])
            mT_b = lgev
            posT_b = sterr2

            P.emit("gpsimd", lambda e: e.dma_start(
                wrT_sb[:], wrT_d.rearrange("(dt p) e -> p dt e", p=128)),
                inc=("semwi", 16))
            P.emit("gpsimd", lambda e: e.dma_start(
                wiT_sb[:], wiT_d.rearrange("(dt p) e -> p dt e", p=128)),
                inc=("semwi", 16))
            P.emit("gpsimd", lambda e: e.dma_start(b2pk[:, 0:D], b2_d), inc=("semwi", 16))
            WRW = B2W = ("semwi", P.cnt["semwi"])
            P.emit("gpsimd", lambda e: e.memset(xbnc[:, 0], 0.0), tag="zm")
            for j in range(SLOTS // 128):
                P.emit("gpsimd", lambda e, j=j: e.dma_start(
                    xsort_d[j * 128:(j + 1) * 128], xbnc[:, 0]),
                    waits=["zm"], inc=("semz", 16))
            ZFW = ("semz", P.cnt["semz"])
            P.emit("sync", lambda e: e.dma_start(eCm1[:], ecm1_d),
                   waits=[CW], inc=("semc", 16))
            ECW = ("semc", P.cnt["semc"])
            P.emit("vector", lambda e: e.memset(ones8[:], 1.0), tag="on8")

            P.emit("scalar", lambda e: e.activation(lgs_t1[:], lam_sb[:], AF.Abs),
                   waits=[CW], tag="ls_ab")
            P.emit("scalar", lambda e: e.activation(lgs_t1[:], lgs_t1[:], AF.Exp,
                                                    scale=-1.0),
                   waits=["ls_ab"], tag="ls_ex")
            P.emit("scalar", lambda e: e.activation(lgs_t1[:], lgs_t1[:], AF.Ln,
                                                    bias=1.0),
                   waits=["ls_ex"], tag="ls_ln")
            P.emit("scalar", lambda e: e.activation(lsig8[:], lam_sb[:], AF.Relu,
                                                    scale=-1.0),
                   waits=["ls_ln"], tag="ls_rl")
            P.emit("vector", lambda e: e.tensor_add(lsig8[:], lsig8[:], lgs_t1[:]),
                   waits=["ls_rl"], tag="ls_sp")
            P.emit("vector", lambda e: e.tensor_scalar_mul(lsig8[:], lsig8[:], -8.0),
                   waits=["ls_sp"], tag="ls_8")
            for dt_ in range(DT):
                P.emit("vector", lambda e, dt_=dt_: e.tensor_scalar(
                    scales[:, dt_, :], mask_sb[:], lsig8[:, dt_:dt_ + 1], None,
                    ALU.mult, ALU.bypass),
                    waits=["ls_8"], tag=f"scl_{dt_}")

            def emit_head(c):
                own = c >= NP
                cb = c % 2
                srct = xown if own else xprev
                coff = (c - NP) * CH if own else c * CH
                xw = []
                if c >= 2:
                    xw += [f"xntd_{c-2}_3", f"sq_{c-2}_3"]
                    if c - 2 >= NP:
                        xw += [f"acc_{c-2}_3"]
                sx = "semx0" if cb == 0 else "semx1"
                xw += [(sx, P.cnt[sx]),
                       ("semacc0", P.cnt["semacc0"]),
                       ("semacc1", P.cnt["semacc1"])]
                P.emit("sync", lambda e, cb=cb, srct=srct, coff=coff: e.dma_start(
                    x_sb[:, cb], srct[coff:coff + CH]
                    .rearrange("(t q) d -> q t d", q=128)),
                    waits=xw, inc=(sx, 16), tag=f"xl_{c}")
                XW = (sx, P.cnt[sx])
                for t in range(4):
                    sw = [XW]
                    if c >= 2:
                        sw += [f"negmr_{c-2}_{t}"]
                    P.emit("vector", lambda e, cb=cb, t=t: e.reduce_sum(
                        stat[:, cb, t, 0:1], x_sb[:, cb, t, :], axis=AX),
                        waits=sw, tag=f"s1_{c}_{t}")
                for t in range(4):
                    sw = [XW, f"sq_{c}_{t-1}" if t else (f"sq_{c-1}_3" if c else None)]
                    if c >= 2:
                        sw += [f"negmr_{c-2}_{t}"]
                    if c >= 1:
                        sw += [f"xnT_{c-1}_7"]
                    if c - 1 >= NP:
                        sw += [("semxn", P.cnt["semxn"])]
                    P.emit("scalar", lambda e, cb=cb, t=t: e.activation(
                        xn_td[:, t, :], x_sb[:, cb, t, :], AF.Square,
                        accum_out=stat[:, cb, t, 1:2]),
                        waits=sw, tag=f"sq_{c}_{t}")
                for t in range(4):
                    P.emit("vector", lambda e, cb=cb, t=t: e.tensor_scalar(
                        stat[:, cb, t, 2:3], stat[:, cb, t, 0:1], 1.0 / D, None,
                        ALU.mult, ALU.bypass),
                        waits=[f"s1_{c}_{t}"], tag=f"m_{c}_{t}")
                    P.emit("vector", lambda e, cb=cb, t=t: e.tensor_mul(
                        stat[:, cb, t, 3:4], stat[:, cb, t, 2:3], stat[:, cb, t, 2:3]),
                        waits=[f"m_{c}_{t}"], tag=f"m2_{c}_{t}")
                    P.emit("vector", lambda e, cb=cb, t=t: e.scalar_tensor_tensor(
                        stat[:, cb, t, 4:5], stat[:, cb, t, 1:2], 1.0 / D,
                        stat[:, cb, t, 3:4], ALU.mult, ALU.subtract),
                        waits=[f"sq_{c}_{t}", f"m2_{c}_{t}"], tag=f"var_{c}_{t}")
                for t in range(4):
                    P.emit("scalar", lambda e, cb=cb, t=t: e.activation(
                        stat[:, cb, t, 5:6], stat[:, cb, t, 4:5], AF.Ln,
                        bias=constv[:, 0:1]),
                        waits=[f"var_{c}_{t}", CONSTW], tag=f"lnv_{c}_{t}")
                    P.emit("scalar", lambda e, cb=cb, t=t: e.activation(
                        stat[:, cb, t, 6:7], stat[:, cb, t, 5:6], AF.Exp,
                        scale=constv[:, 1:2]),
                        waits=[f"lnv_{c}_{t}"], tag=f"rstd_{c}_{t}")
                for t in range(4):
                    P.emit("vector", lambda e, cb=cb, t=t: e.scalar_tensor_tensor(
                        stat[:, cb, t, 7:8], stat[:, cb, t, 2:3], -1.0,
                        stat[:, cb, t, 6:7], ALU.mult, ALU.mult),
                        waits=[f"rstd_{c}_{t}"], tag=f"negmr_{c}_{t}")
                for t in range(4):
                    P.emit("vector", lambda e, cb=cb, t=t: e.tensor_scalar(
                        xn_td[:, t, :], x_sb[:, cb, t, :], stat[:, cb, t, 6:7],
                        stat[:, cb, t, 7:8], ALU.mult, ALU.add),
                        waits=[f"negmr_{c}_{t}", f"sq_{c}_{t}"], tag=f"xntd_{c}_{t}")
                for dt_ in range(DT):
                    pT = p512c if dt_ % 2 == 0 else p512d
                    tw = [f"xntd_{c}_3", CW]
                    if dt_ >= 2:
                        tw += [f"xnf_{c}_{dt_-2}", f"xnb_{c}_{dt_-2}"]
                    elif c >= 1:
                        tw += [f"sgi_{c-1}_{6 + dt_}"]

                    def bxnT(e, pT=pT, dt_=dt_):
                        last = None
                        for t in range(4):
                            last = e.transpose(pT[:, t * 128:(t + 1) * 128],
                                               xn_td[:, t, dt_ * 128:(dt_ + 1) * 128],
                                               ident[:])
                        return last
                    P.emit("tensor", bxnT, waits=tw, tag=f"xnT_{c}_{dt_}")
                    bw = [f"xnT_{c}_{dt_}"]
                    if c >= 1:
                        bw += [f"ich_{c-1}_7"]
                    P.emit("vector", lambda e, pT=pT, dt_=dt_: e.tensor_copy(
                        xnbf_c[:, dt_, :], pT[:]), waits=bw, tag=f"xnb_{c}_{dt_}")
                    fw = [f"xnT_{c}_{dt_}"]
                    if c >= 1:
                        fw += [f"ixn_{c-1}_{dt_}"]
                        if c - 1 >= NP:
                            fw += [f"rtch_{c-1}"]
                    P.emit("vector", lambda e, pT=pT, dt_=dt_: e.tensor_copy(
                        xnf32_c[:, dt_, :], pT[:]), waits=fw, tag=f"xnf_{c}_{dt_}")
                if own:
                    P.emit("gpsimd", lambda e, coff=coff: e.dma_start(
                        xn_tok_d[coff:coff + CH]
                        .rearrange("(t q) d -> q t d", q=128),
                        xn_td[:]),
                        waits=[f"xntd_{c}_{t_}" for t_ in range(4)]
                        + [("semxn", P.cnt["semxn"])],
                        inc=("semxn", 16), tag=f"xnsp_{c}")

            def emit_mid(c):
                own = c >= NP
                for et in range(DT):
                    pr = p512a if et % 2 == 0 else p512b
                    rw = [f"xnb_{c}_7", WRW]
                    rn = c * 8 + et
                    if rn >= 2:
                        pc, pe = divmod(rn - 2, 8)
                        rw += [f"sgr_{pc}_{pe}"]
                    P.emit("tensor", mm_chain(
                        pr[:],
                        [wrT_sb[:, d_, et * 128:(et + 1) * 128] for d_ in range(DT)],
                        [xnbf_c[:, d_, :] for d_ in range(DT)]),
                        waits=rw, tag=f"rch_{c}_{et}")
                    sw = [f"rch_{c}_{et}"]
                    if c >= 1:
                        sw += [f"root_{c-1}_{et}"]
                    P.emit("scalar", lambda e, pr=pr, et=et: e.activation(
                        bT[:, et, :], pr[:], AF.Sigmoid),
                        waits=sw, tag=f"sgr_{c}_{et}")
                for et in range(DT):
                    pi = p512c if et % 2 == 0 else p512d
                    iw = [f"xnb_{c}_7"]
                    if et >= 2:
                        iw += [f"sgi_{c}_{et-2}"]
                    else:
                        iw += [f"xnf_{c}_{6 + et}"]
                    P.emit("tensor", mm_chain(
                        pi[:],
                        [wiT_sb[:, d_, et * 128:(et + 1) * 128] for d_ in range(DT)],
                        [xnbf_c[:, d_, :] for d_ in range(DT)]),
                        waits=iw, tag=f"ich_{c}_{et}")
                    sw = [f"ich_{c}_{et}"]
                    if c >= 1:
                        sw += [f"scan_{c-1}_{et}"]
                    P.emit("scalar", lambda e, pi=pi, et=et: e.activation(
                        bU[:, et, :], pi[:], AF.Sigmoid),
                        waits=sw, tag=f"sgi_{c}_{et}")
                for et in range(DT):
                    P.emit("vector", lambda e, et=et: e.tensor_mul(
                        bU[:, et, :], bU[:, et, :], xnf32_c[:, et, :]),
                        waits=[f"sgi_{c}_{et}", f"xnf_{c}_{et}"], tag=f"ixn_{c}_{et}")
                if own:
                    rw = [f"xnf_{c}_7", CW]
                    if c - 1 >= NP:
                        rw += [f"lgcp_{c-1}"]
                    P.emit("tensor", mm_chain(
                        p1024b[0:NE, 0:512],
                        [wrout_sb[:, d_, :] for d_ in range(DT)],
                        [xnf32_c[:, d_, :] for d_ in range(DT)]),
                        waits=rw, tag=f"rtch_{c}")

            def emit_tail_sv(c):
                for et in range(DT):
                    aw = [f"sgr_{c}_{et}", f"scl_{et}"]
                    if c >= 1:
                        aw += [f"scan_{c-1}_{et}"]
                    P.emit("scalar", lambda e, et=et, c=c: e.activation(
                        bA[:, et, :], bT[:, et, :], AF.Exp,
                        scale=scales[:, et, c:c + 1]), waits=aw,
                        tag=f"a_{c}_{et}")
                for et in range(DT):
                    P.emit("scalar", lambda e, et=et: e.activation(
                        bT[:, et, :], bA[:, et, :], AF.Square),
                        waits=[f"a_{c}_{et}"], tag=f"e2_{c}_{et}")
                for et in range(DT):
                    P.emit("scalar", lambda e, et=et: e.activation(
                        bT[:, et, :], bT[:, et, :], AF.Relu, scale=-1.0, bias=1.0),
                        waits=[f"e2_{c}_{et}"], tag=f"qm_{c}_{et}")
                for et in range(DT):
                    rw2 = [f"qm_{c}_{et}"]
                    if c >= 1:
                        rw2 += [f"cc_{c-1}_{et}"]
                        if c - 1 >= NP:
                            rw2 += [f"hT_{c-1}_3"]
                    P.emit("scalar", lambda e, et=et: e.activation(
                        h_sb[:, et, :], bT[:, et, :], AF.Sqrt),
                        waits=rw2, tag=f"root_{c}_{et}")
                for et in range(DT):
                    P.emit("vector", lambda e, et=et: e.tensor_mul(
                        bU[:, et, :], h_sb[:, et, :], bU[:, et, :]),
                        waits=[f"root_{c}_{et}", f"ixn_{c}_{et}"], tag=f"b_{c}_{et}")
                for et in range(DT):
                    sw = [f"a_{c}_{et}", f"b_{c}_{et}"]
                    if c >= 1:
                        sw += [f"cc_{c-1}_{et}"]
                    init = 0.0 if c == 0 else carry[:, et:et + 1]
                    P.emit("vector", lambda e, et=et, init=init: e.tensor_tensor_scan(
                        h_sb[:, et, :], bA[:, et, :], bU[:, et, :], init,
                        ALU.mult, ALU.add), waits=sw, tag=f"scan_{c}_{et}")
                    P.emit("vector", lambda e, et=et: e.tensor_copy(
                        carry[:, et:et + 1], h_sb[:, et, CH - 1:CH]),
                        waits=[f"scan_{c}_{et}"], tag=f"cc_{c}_{et}")

            def emit_tail_pe(c):
                cb = c % 2
                XW = P.tags[f"xl_{c}"]
                for t in range(4):
                    tw = [f"scan_{c}_7", CW]
                    if t >= 1:
                        tw += [f"acc_{c}_{t-1}"]
                    elif c - 1 >= NP:
                        tw += [f"acc_{c-1}_3"]

                    def bhT(e, t=t):
                        last = None
                        for et in range(DT):
                            last = e.transpose(p1024a[:, et * 128:(et + 1) * 128],
                                               h_sb[:, et, t * 128:(t + 1) * 128],
                                               ident[:])
                        return last
                    P.emit("tensor", bhT, waits=tw, tag=f"hT_{c}_{t}")
                    sa = "semacc0" if t % 2 == 0 else "semacc1"
                    aw = [f"hT_{c}_{t}", XW]
                    P.emit("vector", lambda e, cb=cb, t=t: e.tensor_add(
                        x_sb[:, cb, t, :], x_sb[:, cb, t, :], p1024a[:]),
                        waits=aw, tag=f"acc_{c}_{t}")
                    P.emit("sync", lambda e, c=c, t=t, cb=cb: e.dma_start(
                        acc_d[(c - NP) * 4 + t], x_sb[:, cb, t, :]),
                        waits=[f"acc_{c}_{t}", (sa, P.cnt[sa])],
                        inc=(sa, 16))

            def emit_route(c):
                oc = c - NP
                for j in range(4):
                    tt = 4 * oc + j
                    P.emit("vector", lambda e, tt=tt: e.reduce_max(
                        rstat[:, tt, 0:1], lg_t[:, tt, :], axis=AX),
                        waits=[f"lgcp_{c}"], tag=f"mx_{tt}")
                    P.emit("vector", lambda e, tt=tt: e.tensor_scalar_mul(
                        rstat[:, tt, 1:2], rstat[:, tt, 0:1], -1.0),
                        waits=[f"mx_{tt}"], tag=f"nmx_{tt}")
                    P.emit("scalar", lambda e, tt=tt: e.activation(
                        ex_sc[:], lg_t[:, tt, :], AF.Exp, bias=rstat[:, tt, 1:2]),
                        waits=[f"nmx_{tt}"] + ([f"ex_{tt-1}"] if tt else []),
                        tag=f"ex_{tt}")
                    P.emit("vector", lambda e, tt=tt: e.reduce_sum(
                        rstat[:, tt, 2:3], ex_sc[:], axis=AX),
                        waits=[f"ex_{tt}"], tag=f"sm_{tt}")
                    P.emit("vector", lambda e, tt=tt: e.reciprocal(
                        rstat[:, tt, 3:4], rstat[:, tt, 2:3]),
                        waits=[f"sm_{tt}"], tag=f"gt_{tt}")
                    P.emit("vector", lambda e, tt=tt, j=j: e.tensor_scalar(
                        msk_tok[:, j, :], lg_t[:, tt, :], rstat[:, tt, 0:1],
                        None, ALU.is_equal, ALU.bypass),
                        waits=[f"gt_{tt}"], tag=f"msk_{tt}")
                    P.emit("vector", lambda e, tt=tt, j=j: e.tensor_scalar(
                        s_all[:, tt, :], msk_tok[:, j, :], rstat[:, tt, 3:4],
                        None, ALU.mult, ALU.bypass),
                        waits=[f"msk_{tt}"], tag=f"sall_{tt}")
                tw = [f"sall_{4 * oc + j}" for j in range(4)] + [CW]
                if oc >= 1:
                    tw += [f"poscp_{oc-1}"]

                def bsT(e, oc=oc):
                    last = None
                    for j in range(4):
                        last = e.transpose(
                            p1024b[0:NE, 512 + j * 128:512 + (j + 1) * 128],
                            s_all[:, 4 * oc + j, :], ident[:])
                    return last
                P.emit("tensor", bsT, waits=tw, tag=f"sT_{oc}")
                P.emit("vector", lambda e, oc=oc: e.tensor_copy(
                    sTpk[:, oc * 512:(oc + 1) * 512], p1024b[0:NE, 512:1024]),
                    waits=[f"sT_{oc}"], tag=f"sThc_{oc}")
                P.emit("vector", lambda e, oc=oc: e.tensor_tensor(
                    sterr2[:], p1024b[0:NE, 512:1024],
                    sTpk[:, oc * 512:(oc + 1) * 512], ALU.subtract),
                    waits=[f"sThc_{oc}"], tag=f"sTc_{oc}")
                P.emit("vector", lambda e, oc=oc: e.tensor_copy(
                    sTpk[:, NTT * 128 + oc * 512:NTT * 128 + (oc + 1) * 512],
                    sterr2[:]),
                    waits=[f"sTc_{oc}"], tag=f"sTlc_{oc}")
                P.emit("vector", lambda e: e.tensor_scalar(
                    mT_b[:, 0:512], p1024b[0:NE, 512:1024], 0.0, None,
                    ALU.is_gt, ALU.bypass),
                    waits=[f"sTlc_{oc}", f"lgtr_{c}"], tag=f"mT_{oc}")
                init = 0.0 if oc == 0 else cnt8[:, 0:1]
                P.emit("vector", lambda e, init=init: e.tensor_tensor_scan(
                    posT_b[:], ones8[:], mT_b[:, 0:512], init, ALU.mult, ALU.add),
                    waits=[f"mT_{oc}", "on8"]
                    + ([f"cnt8_{oc-1}"] if oc else []), tag=f"posT_{oc}")
                P.emit("vector", lambda e: e.tensor_copy(
                    cnt8[:], posT_b[:, 511:512]),
                    waits=[f"posT_{oc}"], tag=f"cnt8_{oc}")

                def bpT(e):
                    last = None
                    for j in range(4):
                        last = e.transpose(
                            p1024b[:, 512 + j * NE:512 + (j + 1) * NE],
                            posT_b[:, j * 128:(j + 1) * 128],
                            ident[0:NE, 0:NE])
                    return last
                P.emit("tensor", bpT, waits=[f"posT_{oc}", f"mT_{oc}"],
                       tag=f"posbT_{oc}")
                P.emit("vector", lambda e: e.tensor_copy(
                    pos_tok[:].rearrange("p t e -> p (t e)"),
                    p1024b[:, 512:512 + 4 * NE]),
                    waits=[f"posbT_{oc}"], tag=f"poscp_{oc}")
                for j in range(4):
                    tt = 4 * oc + j
                    P.emit("vector", lambda e, j=j: e.tensor_scalar(
                        dtmp[:], pos_tok[:, j, :], float(C_CAP), None,
                        ALU.min, ALU.bypass),
                        waits=[f"poscp_{oc}", f"dg_{tt-1}" if tt else None],
                        tag=f"da_{tt}")
                    P.emit("vector", lambda e: e.tensor_add(
                        dtmp[:], dtmp[:], eCm1[:]),
                        waits=[f"da_{tt}", ECW], tag=f"db_{tt}")
                    P.emit("vector", lambda e, j=j: e.tensor_mul(
                        dtmp[:], dtmp[:], msk_tok[:, j, :]),
                        waits=[f"db_{tt}"], tag=f"dc_{tt}")
                    P.emit("vector", lambda e, j=j: e.reduce_sum(
                        dest_f4[:, j:j + 1], dtmp[:], axis=AX),
                        waits=[f"dc_{tt}"], tag=f"dg_{tt}")
                P.emit("vector", lambda e, oc=oc: e.tensor_copy(
                    dest_i[:, oc * 4:(oc + 1) * 4], dest_f4[:]),
                    waits=[f"dg_{4 * oc + 3}"], tag=f"dsti_{oc}")
                xnspW = P.tags[f"xnsp_{c}"]
                for hb in range(2):
                    scprev = ("semsc", P.cnt["semsc"])
                    P.emit("sync", lambda e, oc=oc, hb=hb: e.dma_start(
                        xbnc[:], xn_tok_d[oc * 512 + hb * 256:
                                          oc * 512 + (hb + 1) * 256]
                        .rearrange("(t q2) d -> q2 t d", q2=128)),
                        waits=[xnspW, scprev, ("semt0", P.cnt["semt0"])]
                        + ([ZFW] if oc == 0 and hb == 0 else []),
                        inc=("semt0", 16))
                    XBW = ("semt0", P.cnt["semt0"])
                    for j2 in range(2):
                        tt = 4 * oc + hb * 2 + j2
                        P.emit("gpsimd", lambda e, tt=tt, j2=j2:
                               e.indirect_dma_start(
                                   xsort_d,
                                   bass.IndirectOffsetOnAxis(
                                       dest_i[:, tt:tt + 1], 0),
                                   xbnc[:, j2], None),
                               waits=[ZFW, XBW, f"dsti_{oc}"],
                               inc=("semsc", 16))

            def emit_lg(c):
                coff = (c - NP) * CH
                rtw = [f"rtch_{c}", CW]
                if c - 1 >= NP:
                    rtw += [f"lgtr_{c-1}", f"posT_{c-1-NP}"]
                P.emit("vector", lambda e: e.tensor_scalar(
                    lgev[:], p1024b[0:NE, 0:512], br_sb[:, 0:1], None,
                    ALU.add, ALU.bypass),
                    waits=rtw, tag=f"rt_{c}")

                def blgtr(e):
                    last = None
                    for t in range(4):
                        last = e.transpose(p1024b[:, t * NE:(t + 1) * NE],
                                           lgev[:, t * 128:(t + 1) * 128],
                                           ident[0:NE, 0:NE])
                    return last
                P.emit("tensor", blgtr, waits=[f"rt_{c}"], tag=f"lgtr_{c}")
                P.emit("vector", lambda e, coff=coff: e.tensor_copy(
                    lg_t[:, coff // 128:coff // 128 + 4, :]
                    .rearrange("p t e -> p (t e)"),
                    p1024b[:, 0:4 * NE]),
                    waits=[f"lgtr_{c}"], tag=f"lgcp_{c}")

            for c in range(NCH + 1):
                if c < NCH:
                    emit_head(c)
                if c >= 1:
                    emit_tail_sv(c - 1)
                    if c - 1 >= NP:
                        emit_tail_pe(c - 1)
                        emit_lg(c - 1)
                        emit_route(c - 1)
                if c < NCH:
                    emit_mid(c)

            P.tags["desti"] = P.tags[f"dsti_{NO - 1}"]
            ACCW0 = ("semacc0", P.cnt["semacc0"])
            ACCW1 = ("semacc1", P.cnt["semacc1"])
            XNW = ("semxn", P.cnt["semxn"])
            endA = [(s, P.cnt[s]) for s in
                    ["sTE", "sDVE", "sACT", "semxn", "semacc0", "semacc1",
                     "semx0", "semx1", "semwi"]]

            with nc.Block() as blkA:
                P.flush(blkA)

        with ExitStack() as phB:
            def bsb(n_, s, dt=F32):
                return phB.enter_context(nc.sbuf_tensor(n_, s, dt))
            sterr = bsb("sterr", [NE, 512])
            ygt = bsb("ygt", [128, 4, 1024])
            yacc = bsb("yacc", [128, 16, 1024])
            w1c = bsb("w1c", [128, 2, DT, 512], BF16)
            w2c = bsb("w2c", [128, 2, 4, 1024], BF16)
            h1m = bsb("h1m", [128, 2, 4, CH], BF16)
            xse = bsb("xse", [128, 2, DT, C_CAP], BF16)

            for j in range(2):
                P.emit("vector", lambda e, j=j: e.tensor_sub(
                    sterr[:], b2_sb[:, j * 512:(j + 1) * 512],
                    b2pk[:, j * 512:(j + 1) * 512]),
                    waits=endA + [CW, B2W] + ([f"b2l_{j-1}"] if j else []),
                    tag=f"b2e_{j}")
                P.emit("vector", lambda e, j=j: e.tensor_copy(
                    b2pk[:, D + j * 512:D + (j + 1) * 512], sterr[:]),
                    waits=[f"b2e_{j}"], tag=f"b2l_{j}")


            SCW = ("semsc", P.cnt["semsc"])

            fi_n = 0
            g_n = 0
            tt_n = 0
            ytag = []
            ylast = {}
            wdone = []
            yprev = {}
            hc_by_exp = []
            yspart = {}
            for q in range(NQ):
                for e2 in range(EPQ):
                    e_ = q * EPQ + e2
                    xpar = e_ % 2
                    sxs = "semxs0" if xpar == 0 else "semxs1"
                    xw2 = [SCW, (sxs, P.cnt[sxs])]
                    if e_ >= 2:
                        xw2 += [hc_by_exp[e_ - 2]]
                    else:
                        xw2 += endA
                    P.emit("sync", lambda e, e_=e_, xpar=xpar: e.dma_start_transpose(
                        xse[:, xpar], xsort_d[e_ * C_CAP:(e_ + 1) * C_CAP]),
                        waits=xw2, inc=(sxs, 16))
                    XSW = (sxs, P.cnt[sxs])
                    for fc in range(NFC2):
                        k = e_ * NFC2 + fc
                        kb = k % 2
                        swp = "semw0" if kb == 0 else "semw1"
                        ww = [wdone[k - 2]] if k >= 2 else list(endA)
                        ww += [(swp, P.cnt[swp])]
                        P.emit("gpsimd", lambda e, e_=e_, fc=fc, kb=kb: e.dma_start(
                            w1c[:, kb], w1[e_, :, fc * 512:(fc + 1) * 512]
                            .rearrange("(dt q2) f -> q2 dt f", q2=128)),
                            waits=ww, inc=(swp, 16))
                        P.emit("gpsimd", lambda e, e_=e_, fc=fc, kb=kb: e.dma_start(
                            w2c[:, kb], w2[e_, fc * 512:(fc + 1) * 512, :]
                            .rearrange("(ft q2) n -> q2 ft n", q2=128)),
                            inc=(swp, 16))
                        wlev = (swp, P.cnt[swp])

                        def emit_Hpair():
                            nonlocal fi_n
                            for fi in range(4):
                                pA, pB = ((p512a, p512b) if fi_n % 2 == 0
                                          else (p512c, p512d))
                                hw = [wlev, XSW]
                                if fi_n >= 2:
                                    hw += [f"heB_{fi_n - 2}"]

                                def bH(e, pA=pA, pB=pB, fi=fi, kb=kb, xpar=xpar):
                                    last = None
                                    for d_ in range(DT):
                                        w = w1c[:, kb, d_, fi * 128:(fi + 1) * 128]
                                        last = e.matmul(
                                            pA[:], w, xse[:, xpar, d_, 0:CH],
                                            start=(d_ == 0), stop=(d_ == DT - 1))
                                        last = e.matmul(
                                            pB[:], w, xse[:, xpar, d_, CH:2 * CH],
                                            start=(d_ == 0), stop=(d_ == DT - 1))
                                    return last
                                P.emit("tensor", bH, waits=hw, tag=f"hc_{fi_n}")
                                col = e_ * NFC * 8 + fc * 4 + fi
                                swA = [f"hc_{fi_n}"]
                                swB = [f"hc_{fi_n}"]
                                if 0 in ylast:
                                    swA += [ylast[0]]
                                if 1 in ylast:
                                    swB += [ylast[1]]
                                P.emit("scalar", lambda e, pA=pA, fi=fi, col=col:
                                       e.activation(h1m[:, 0, fi, :], pA[:],
                                                    AF.Relu,
                                                    bias=b1_sb[:, col:col + 1]),
                                       waits=swA, tag=f"heA_{fi_n}")
                                P.emit("scalar", lambda e, pB=pB, fi=fi, col=col:
                                       e.activation(h1m[:, 1, fi, :], pB[:],
                                                    AF.Relu,
                                                    bias=b1_sb[:, col:col + 1]),
                                       waits=swB, tag=f"heB_{fi_n}")
                                fi_n += 1

                        def emit_Y(g):
                            nonlocal tt_n
                            helast = (f"heA_{fi_n - 1}" if g == 0
                                      else f"heB_{fi_n - 1}")
                            lastyc = None
                            for tti in range(4):
                                tile = e2 * TPE + g * 4 + tti
                                py = p1024a if tt_n % 2 == 0 else p1024b
                                yw2 = [helast]
                                if tt_n >= 2:
                                    yw2 += [ytag[tt_n - 2]]

                                def bY(e, py=py, g=g, tti=tti, kb=kb):
                                    last = None
                                    for fi in range(4):
                                        for dh in range(2):
                                            last = e.matmul(
                                                py[:, dh * 512:(dh + 1) * 512],
                                                h1m[:, g, fi,
                                                    tti * 128:(tti + 1) * 128],
                                                w2c[:, kb, fi,
                                                    dh * 512:(dh + 1) * 512],
                                                start=(fi == 0), stop=(fi == 3))
                                    return last
                                P.emit("tensor", bY, waits=yw2, tag=f"yc_{tt_n}_1")
                                vw = [f"yc_{tt_n}_1"]
                                if fc == 0 and tile in yspart:
                                    vw += [yspart[tile]]
                                if tile in yprev:
                                    vw += [yprev[tile]]
                                if fc == 0:
                                    P.emit("vector", lambda e, py=py, tile=tile:
                                           e.tensor_copy(yacc[:, tile, :], py[:]),
                                           waits=vw, tag=f"ye_{tt_n}")
                                else:
                                    P.emit("vector", lambda e, py=py, tile=tile:
                                           e.tensor_add(yacc[:, tile, :],
                                                        yacc[:, tile, :], py[:]),
                                           waits=vw, tag=f"ye_{tt_n}")
                                ytag.append(f"ye_{tt_n}")
                                yprev[tile] = f"ye_{tt_n}"
                                lastyc = f"yc_{tt_n}_1"
                                tt_n += 1
                            ylast[g] = lastyc

                        assert NGE == 2
                        emit_Hpair()
                        emit_Y(0)
                        emit_Y(1)
                        wdone.append(f"yc_{tt_n - 1}_1")
                    hc_by_exp.append(f"hc_{fi_n - 1}")
                for part in range(4):
                    sp = ["semys", "semt1", "semx0", "semx1"][part]
                    P.emit("scalar", lambda e, q=q, part=part: e.dma_start(
                        ysort_d[q * 2048 + part * 512:q * 2048 + (part + 1) * 512]
                        .rearrange("(t q2) d -> q2 t d", q2=128),
                        yacc[:, part * 4:(part + 1) * 4, :]),
                        waits=[yprev[part * 4 + j] for j in range(4)]
                        + [(sp, P.cnt[sp])],
                        inc=(sp, 16))
                    for j in range(4):
                        yspart[part * 4 + j] = (sp, P.cnt[sp])
                yprev = {}
            YSF4 = [(sp, P.cnt[sp])
                    for sp in ["semys", "semt1", "semx0", "semx1"]]

            for p in range(2):
                ypart = {}
                for part in range(NTTH // 4):
                    yw = [ACCW0, ACCW1, ("semY", P.cnt["semY"])] + YSF4 + endA
                    if p == 1:
                        yw += [("semout", P.cnt["semout"])]
                    P.emit("sync", lambda e, p=p, part=part: e.dma_start(
                        yacc[:, part * 4:(part + 1) * 4, :],
                        acc_d[p * NTTH + part * 4:p * NTTH + (part + 1) * 4]
                        .rearrange("t q2 d -> q2 t d")),
                        waits=yw, inc=("semY", 16))
                    for j in range(4):
                        ypart[part * 4 + j] = ("semY", P.cnt["semY"])
                outparts = []
                for tt in range(NTTH):
                    ttg = p * NTTH + tt
                    par = tt % 4
                    gw = [("semgt", P.cnt["semgt"]), "desti"] + YSF4
                    if ttg >= 4:
                        gw += [f"yga_{ttg - 4}"]
                    P.emit("gpsimd", lambda e, ttg=ttg, par=par: e.indirect_dma_start(
                        ygt[:, par], None,
                        ysort_d, bass.IndirectOffsetOnAxis(
                            dest_i[:, ttg:ttg + 1], 0)),
                        waits=gw, inc=("semgt", 16))
                    GW = ("semgt", P.cnt["semgt"])
                    py = p1024a if tt_n % 2 == 0 else p1024b
                    for dh in range(2):
                        bw = [f"sTlc_{NTT // 4 - 1}", "b2l_1"]
                        if dh == 0 and tt_n >= 2:
                            bw += [ytag[tt_n - 2]]
                        P.emit("tensor", mm_chain(
                            py[:, dh * 512:(dh + 1) * 512],
                            [sTpk[:, ttg * 128:(ttg + 1) * 128],
                             sTpk[:, ttg * 128:(ttg + 1) * 128],
                             sTpk[:, NTT * 128 + ttg * 128:
                                  NTT * 128 + (ttg + 1) * 128],
                             sTpk[:, NTT * 128 + ttg * 128:
                                  NTT * 128 + (ttg + 1) * 128]],
                            [b2pk[:, dh * 512:(dh + 1) * 512],
                             b2pk[:, D + dh * 512:D + (dh + 1) * 512],
                             b2pk[:, dh * 512:(dh + 1) * 512],
                             b2pk[:, D + dh * 512:D + (dh + 1) * 512]]),
                            waits=bw, tag=f"b2c_{tt_n}_{dh}")
                    vw = [GW, ypart[tt], f"gt_{ttg}"]
                    P.emit("vector", lambda e, tt=tt, ttg=ttg, par=par:
                           e.scalar_tensor_tensor(
                               yacc[:, tt, :], ygt[:, par],
                               rstat[:, ttg, 3:4], yacc[:, tt, :],
                               ALU.mult, ALU.add),
                           waits=vw, tag=f"yga_{ttg}")
                    P.emit("vector", lambda e, py=py, tt=tt: e.tensor_add(
                        yacc[:, tt, :], yacc[:, tt, :], py[:]),
                        waits=[f"b2c_{tt_n}_0", f"b2c_{tt_n}_1", f"yga_{ttg}"],
                        tag=f"b2a_{tt_n}")
                    ytag.append(f"b2a_{tt_n}")
                    lastb2 = f"b2a_{tt_n}"
                    tt_n += 1
                    if tt % 4 == 3:
                        P.emit("scalar", lambda e, p=p, tt=tt: e.dma_start(
                            out_d[p * HALF_T + (tt - 3) * 128:
                                  p * HALF_T + (tt + 1) * 128].rearrange(
                                "(t q2) d -> q2 t d", q2=128),
                            yacc[:, tt - 3:tt + 1, :]),
                            waits=[f"b2a_{tt_n - 1 - j}" for j in range(4)]
                            + [("semout", P.cnt["semout"])],
                            inc=("semout", 16))

            P.emit("sync", lambda e: e.nop(),
                   waits=[("semout", P.cnt["semout"])])

            with nc.Block() as blkB:
                P.flush(blkB)

    return nc


def kernel(L0_attn_x, L0_attn_W_r, L0_attn_W_i, L0_attn_lambd,
           L0_ffn_Wr, L0_ffn_br, L0_ffn_W1, L0_ffn_b1, L0_ffn_W2, L0_ffn_b2):
    x = np.ascontiguousarray(np.asarray(L0_attn_x, dtype=np.float32))
    assert x.shape == (4, 8192, 1024)
    wrT = np.ascontiguousarray(np.asarray(L0_attn_W_r, np.float32).T)
    wiT = np.ascontiguousarray(np.asarray(L0_attn_W_i, np.float32).T)
    lam8 = np.ascontiguousarray(np.asarray(L0_attn_lambd, np.float32).reshape(8, 128).T)
    wrout = np.ascontiguousarray(np.asarray(L0_ffn_Wr, np.float32))
    br = np.ascontiguousarray(np.asarray(L0_ffn_br, np.float32).reshape(8, 1))
    w1 = np.ascontiguousarray(np.asarray(L0_ffn_W1, np.float32))
    b1r = np.ascontiguousarray(
        np.asarray(L0_ffn_b1, np.float32).reshape(8, 32, 128).transpose(2, 0, 1)
        .reshape(128, 256))
    w2 = np.ascontiguousarray(np.asarray(L0_ffn_W2, np.float32))
    b2 = np.ascontiguousarray(np.asarray(L0_ffn_b2, np.float32))
    ident = np.eye(128, dtype=np.float32)
    ecm1 = np.ascontiguousarray(np.broadcast_to(
        np.arange(8, dtype=np.float32) * 1024.0 - 1.0, (128, 8)).astype(np.float32))

    nc = build_nc()
    in_maps = []
    for core in range(8):
        b, hh = core // 2, core % 2
        xo = np.ascontiguousarray(x[b, hh * 4096:(hh + 1) * 4096])
        xp = np.ascontiguousarray(x[b, 0:4096]) if hh == 1 else xo
        maskv = np.ones((128, 16), dtype=np.float32)
        if hh == 0:
            maskv[:, 0:8] = 0.0
        in_maps.append(dict(
            xprev=xp, xown=xo, wrT=wrT, wiT=wiT, lam8=lam8, maskv=maskv,
            wrout=wrout, br=br, w1=w1, b1r=b1r, w2=w2, b2=b2, ident=ident,
            ecm1=ecm1))

    res = run_bass_kernel_spmd(nc, in_maps, core_ids=list(range(8)))
    out = np.empty((4, 8192, 1024), dtype=np.float32)
    for core in range(8):
        b, hh = core // 2, core % 2
        out[b, hh * 4096:(hh + 1) * 4096] = res.results[core]["out"]
    return out


# version 23 (final)
# speedup vs baseline: 1.1921x; 1.0039x over previous
import numpy as np
from contextlib import ExitStack
import concourse.bass as bass
import concourse.mybir as mybir
from concourse.bass_utils import run_bass_kernel_spmd

F32 = mybir.dt.float32
BF16 = mybir.dt.bfloat16
AF = mybir.ActivationFunctionType
ALU = mybir.AluOpType
AX = mybir.AxisListType.X

CFG = dict(NP=8, NO=8, NFC=4, NE=8)
D = 1024
DT = 8
CH = 512

ENG_SEM = {"tensor": "sTE", "vector": "sDVE", "scalar": "sACT", "gpsimd": "sGPC",
           "sync": "sSYN"}


class Prog:

    def __init__(self, nc):
        self.nc = nc
        self.semh = {}
        self.cnt = {}
        self.tags = {}
        self.ops = []
        self.maxw = {}

    def add_sem(self, stack, name):
        self.semh[name] = stack.enter_context(self.nc.semaphore(name))
        self.cnt[name] = 0

    def emit(self, eng, build, waits=(), tag=None, inc=None):
        wl = []
        for w in waits:
            if w is None:
                continue
            wl.append(w if isinstance(w, tuple) else self.tags[w])
        if inc is None:
            inc = (ENG_SEM[eng], 1)
        self.ops.append((eng, wl, build, inc))
        self.cnt[inc[0]] += inc[1]
        if tag is not None:
            self.tags[tag] = (inc[0], self.cnt[inc[0]])

    def flush(self, block):
        by_eng = {}
        for op in self.ops:
            by_eng.setdefault(op[0], []).append(op)
        for engname, engops in by_eng.items():
            mw = self.maxw.setdefault(engname, {})

            def body(e, engops=engops, mw=mw):
                for (_, wl, build, inc) in engops:
                    for (s, v) in wl:
                        if v > mw.get(s, 0):
                            e.wait_ge(self.semh[s], v)
                            mw[s] = v
                    inst = build(e)
                    if inst is not None:
                        inst.then_inc(self.semh[inc[0]], inc[1])
            getattr(block, engname)(body)
        self.ops = []


def mm_chain(psum_ap, stat_list, mov_list):
    def b(e, psum_ap=psum_ap, stat_list=list(stat_list), mov_list=list(mov_list)):
        last = None
        n = len(stat_list)
        for i in range(n):
            last = e.matmul(psum_ap, stat_list[i], mov_list[i],
                            start=(i == 0), stop=(i == n - 1))
        return last
    return b


def build_nc(cfg=None):
    cfg = dict(CFG if cfg is None else cfg)
    NP, NO, NFC, NE = cfg["NP"], cfg["NO"], cfg["NFC"], cfg["NE"]
    NCH = NP + NO
    TOWN = NO * CH
    TPRE = NP * CH
    DFF = NFC * 1024
    HALF_T = TOWN // 2
    NG = HALF_T // CH
    NTTH = HALF_T // 128
    NTT = TOWN // 128
    assert NO % 2 == 0 and NG >= 1 and NTT % 4 == 0

    nc = bass.Bass("TRN2", target_bir_lowering=False)

    def dr(n_, s, dt=F32, kind="ExternalInput"):
        return nc.dram_tensor(n_, s, dt, kind=kind).ap()

    xprev = dr("xprev", [TPRE, D])
    xown = dr("xown", [TOWN, D])
    wrT_d = dr("wrT", [D, D])
    wiT_d = dr("wiT", [D, D])
    lam8 = dr("lam8", [128, DT])
    maskv = dr("maskv", [128, NCH])
    wrout_d = dr("wrout", [D, NE])
    br_d = dr("br", [NE, 1])
    w1 = dr("w1", [NE, D, DFF])
    b1_d = dr("b1r", [128, NE * NFC * 8])
    w2 = dr("w2", [NE, DFF, D])
    b2_d = dr("b2", [NE, D])
    ident_d = dr("ident", [128, 128])
    ecm1_d = dr("ecm1", [128, NE])
    out_d = dr("out", [TOWN, D], F32, kind="ExternalOutput")
    acc_d = dr("acc_d", [NTT, 128, D], F32, kind="Internal")
    SLOTS = 2 * TOWN
    C_CAP = SLOTS // NE
    NQ = SLOTS // 2048
    EPQ = NE // NQ
    TPE = C_CAP // 128
    NGE = C_CAP // 512
    NFC2 = DFF // 512
    assert C_CAP % 512 == 0 and EPQ * C_CAP == 2048 and NGE >= 1
    xn_tok_d = nc.dram_tensor("xn_tok_d", [TOWN, D], BF16, kind="Internal").ap()
    xsort_d = nc.dram_tensor("xsort_d", [SLOTS, D], BF16, kind="Internal").ap()
    ysort_d = nc.dram_tensor("ysort_d", [SLOTS, D], F32, kind="Internal").ap()

    with ExitStack() as outer:
        P = Prog(nc)
        for s in ["semx0", "semx1", "semxn", "semacc0", "semacc1", "semY",
                  "semz", "semt0", "semt1", "semsc", "semxs0", "semxs1",
                  "semgt", "semys", "semout", "semc", "semw0", "semw1",
                  "semwi", "sTE", "sDVE", "sACT", "sGPC", "sSYN"]:
            P.add_sem(outer, s)

        def sb(n_, s, dt=F32):
            return outer.enter_context(nc.sbuf_tensor(n_, s, dt))

        def psum(n_, s):
            return outer.enter_context(nc.psum_tensor(n_, s, F32))

        ident = sb("ident_s", [128, 128])
        lam_sb = sb("lam_sb", [128, DT])
        mask_sb = sb("mask_sb", [128, NCH])
        scales = sb("scales", [128, DT, NCH])
        lsig8 = sb("lsig8", [128, DT])
        lgs_t1 = sb("lgs_t1", [128, DT])
        br_sb = sb("br_sb", [NE, 1])
        wrout_sb = sb("wrout_sb", [128, DT, NE])
        carry = sb("carry", [128, DT])
        b1_sb = sb("b1_sb", [128, NE * NFC * 8])
        b2_sb = sb("b2_sb", [NE, D])
        b2pk = sb("b2pk", [NE, 2 * D], BF16)
        s_all = sb("s_all", [128, NTT, NE])
        lg_t = sb("lg_t", [128, NTT, NE])
        sTpk = sb("sTpk", [NE, 2 * NTT * 128], BF16)
        rstat = sb("rstat", [128, NTT, 4])
        constv = sb("constv", [128, 8])
        dest_i = sb("dest_i", [128, NTT], mybir.dt.int32)
        ex_sc = sb("ex_sc", [128, NE])
        xbnc = sb("xbnc", [128, 2, 1024], BF16)

        p512a = psum("p512a", [128, 512])
        p512b = psum("p512b", [128, 512])
        p512c = psum("p512c", [128, 512])
        p512d = psum("p512d", [128, 512])
        p1024a = psum("p1024a", [128, 1024])
        p1024b = psum("p1024b", [128, 1024])

        for j, v in enumerate([1e-5, -0.5, 2.0, 1e-38, 0.5]):
            P.emit("gpsimd", lambda e, j=j, v=v: e.memset(constv[:, j:j + 1], v))
        CONSTW = ("sGPC", P.cnt["sGPC"])

        for dst, src in [(ident[:], ident_d), (lam_sb[:], lam8),
                         (mask_sb[:], maskv), (br_sb[:], br_d),
                         (wrout_sb[:], wrout_d.rearrange("(dt p) e -> p dt e", p=128)),
                         (b1_sb[:], b1_d), (b2_sb[:], b2_d)]:
            P.emit("sync", lambda e, d=dst, s=src: e.dma_start(d, s), inc=("semc", 16))
        CW = ("semc", P.cnt["semc"])

        with ExitStack() as phA:
            def asb(n_, s, dt=F32):
                return phA.enter_context(nc.sbuf_tensor(n_, s, dt))
            x_sb = asb("x_sb", [128, 2, 4, 1024])
            xn_td = asb("xn_td", [128, 4, 1024])
            xnbf_c = asb("xnbf_c", [128, DT, CH], BF16)
            xnf32_c = asb("xnf32_c", [128, DT, CH])
            wrT_sb = asb("wrT_sb", [128, DT, D], BF16)
            wiT_sb = asb("wiT_sb", [128, DT, D], BF16)
            bA = asb("bA", [128, DT, CH])
            bT = asb("bT", [128, DT, CH])
            bU = asb("bU", [128, DT, CH])
            h_sb = asb("h_sb", [128, DT, CH])
            lgev = asb("lgev", [NE, CH])
            stat = asb("stat", [128, 2, 4, 8])
            msk_tok = asb("msk_tok", [128, 4, NE])
            pos_tok = asb("pos_tok", [128, 4, NE])
            dest_f4 = asb("dest_f4", [128, 4])
            dtmp = asb("dtmp", [128, NE])
            eCm1 = asb("eCm1", [128, NE])
            ones8 = asb("ones8", [NE, 512], BF16)
            cnt8 = asb("cnt8", [NE, 1])
            sterr2 = asb("sterr2", [NE, 512])
            mT_b = lgev
            posT_b = sterr2

            P.emit("gpsimd", lambda e: e.dma_start(
                wrT_sb[:], wrT_d.rearrange("(dt p) e -> p dt e", p=128)),
                inc=("semwi", 16))
            P.emit("gpsimd", lambda e: e.dma_start(
                wiT_sb[:], wiT_d.rearrange("(dt p) e -> p dt e", p=128)),
                inc=("semwi", 16))
            P.emit("gpsimd", lambda e: e.dma_start(b2pk[:, 0:D], b2_d), inc=("semwi", 16))
            WRW = B2W = ("semwi", P.cnt["semwi"])
            P.emit("gpsimd", lambda e: e.memset(xbnc[:, 0], 0.0), tag="zm")
            for j in range(SLOTS // 128):
                P.emit("gpsimd", lambda e, j=j: e.dma_start(
                    xsort_d[j * 128:(j + 1) * 128], xbnc[:, 0]),
                    waits=["zm"], inc=("semz", 16))
            ZFW = ("semz", P.cnt["semz"])
            P.emit("sync", lambda e: e.dma_start(eCm1[:], ecm1_d),
                   waits=[CW], inc=("semc", 16))
            ECW = ("semc", P.cnt["semc"])
            P.emit("vector", lambda e: e.memset(ones8[:], 1.0), tag="on8")

            P.emit("scalar", lambda e: e.activation(lgs_t1[:], lam_sb[:], AF.Abs),
                   waits=[CW], tag="ls_ab")
            P.emit("scalar", lambda e: e.activation(lgs_t1[:], lgs_t1[:], AF.Exp,
                                                    scale=-1.0),
                   waits=["ls_ab"], tag="ls_ex")
            P.emit("scalar", lambda e: e.activation(lgs_t1[:], lgs_t1[:], AF.Ln,
                                                    bias=1.0),
                   waits=["ls_ex"], tag="ls_ln")
            P.emit("scalar", lambda e: e.activation(lsig8[:], lam_sb[:], AF.Relu,
                                                    scale=-1.0),
                   waits=["ls_ln"], tag="ls_rl")
            P.emit("vector", lambda e: e.tensor_add(lsig8[:], lsig8[:], lgs_t1[:]),
                   waits=["ls_rl"], tag="ls_sp")
            P.emit("vector", lambda e: e.tensor_scalar_mul(lsig8[:], lsig8[:], -8.0),
                   waits=["ls_sp"], tag="ls_8")
            for dt_ in range(DT):
                P.emit("vector", lambda e, dt_=dt_: e.tensor_scalar(
                    scales[:, dt_, :], mask_sb[:], lsig8[:, dt_:dt_ + 1], None,
                    ALU.mult, ALU.bypass),
                    waits=["ls_8"], tag=f"scl_{dt_}")

            def emit_head(c):
                own = c >= NP
                cb = c % 2
                srct = xown if own else xprev
                coff = (c - NP) * CH if own else c * CH
                xw = []
                if c >= 2:
                    xw += [f"xntd_{c-2}_3", f"sq_{c-2}_3"]
                    if c - 2 >= NP:
                        xw += [f"acc_{c-2}_3"]
                sx = "semx0" if cb == 0 else "semx1"
                xw += [(sx, P.cnt[sx]),
                       ("semacc0", P.cnt["semacc0"]),
                       ("semacc1", P.cnt["semacc1"])]
                P.emit("sync", lambda e, cb=cb, srct=srct, coff=coff: e.dma_start(
                    x_sb[:, cb], srct[coff:coff + CH]
                    .rearrange("(t q) d -> q t d", q=128)),
                    waits=xw, inc=(sx, 16), tag=f"xl_{c}")
                XW = (sx, P.cnt[sx])
                for t in range(4):
                    sw = [XW]
                    if c >= 2:
                        sw += [f"negmr_{c-2}_{t}"]
                    P.emit("vector", lambda e, cb=cb, t=t: e.reduce_sum(
                        stat[:, cb, t, 0:1], x_sb[:, cb, t, :], axis=AX),
                        waits=sw, tag=f"s1_{c}_{t}")
                for t in range(4):
                    sw = [XW, f"sq_{c}_{t-1}" if t else (f"sq_{c-1}_3" if c else None)]
                    if c >= 2:
                        sw += [f"negmr_{c-2}_{t}"]
                    if c >= 1:
                        sw += [f"xnT_{c-1}_7"]
                    if c - 1 >= NP:
                        sw += [("semxn", P.cnt["semxn"])]
                    P.emit("scalar", lambda e, cb=cb, t=t: e.activation(
                        xn_td[:, t, :], x_sb[:, cb, t, :], AF.Square,
                        accum_out=stat[:, cb, t, 1:2]),
                        waits=sw, tag=f"sq_{c}_{t}")
                for t in range(4):
                    P.emit("vector", lambda e, cb=cb, t=t: e.tensor_scalar(
                        stat[:, cb, t, 2:3], stat[:, cb, t, 0:1], 1.0 / D, None,
                        ALU.mult, ALU.bypass),
                        waits=[f"s1_{c}_{t}"], tag=f"m_{c}_{t}")
                    P.emit("vector", lambda e, cb=cb, t=t: e.tensor_mul(
                        stat[:, cb, t, 3:4], stat[:, cb, t, 2:3], stat[:, cb, t, 2:3]),
                        waits=[f"m_{c}_{t}"], tag=f"m2_{c}_{t}")
                    P.emit("vector", lambda e, cb=cb, t=t: e.scalar_tensor_tensor(
                        stat[:, cb, t, 4:5], stat[:, cb, t, 1:2], 1.0 / D,
                        stat[:, cb, t, 3:4], ALU.mult, ALU.subtract),
                        waits=[f"sq_{c}_{t}", f"m2_{c}_{t}"], tag=f"var_{c}_{t}")
                for t in range(4):
                    P.emit("scalar", lambda e, cb=cb, t=t: e.activation(
                        stat[:, cb, t, 5:6], stat[:, cb, t, 4:5], AF.Ln,
                        bias=constv[:, 0:1]),
                        waits=[f"var_{c}_{t}", CONSTW], tag=f"lnv_{c}_{t}")
                    P.emit("scalar", lambda e, cb=cb, t=t: e.activation(
                        stat[:, cb, t, 6:7], stat[:, cb, t, 5:6], AF.Exp,
                        scale=constv[:, 1:2]),
                        waits=[f"lnv_{c}_{t}"], tag=f"rstd_{c}_{t}")
                for t in range(4):
                    P.emit("vector", lambda e, cb=cb, t=t: e.scalar_tensor_tensor(
                        stat[:, cb, t, 7:8], stat[:, cb, t, 2:3], -1.0,
                        stat[:, cb, t, 6:7], ALU.mult, ALU.mult),
                        waits=[f"rstd_{c}_{t}"], tag=f"negmr_{c}_{t}")
                for t in range(4):
                    P.emit("vector", lambda e, cb=cb, t=t: e.tensor_scalar(
                        xn_td[:, t, :], x_sb[:, cb, t, :], stat[:, cb, t, 6:7],
                        stat[:, cb, t, 7:8], ALU.mult, ALU.add),
                        waits=[f"negmr_{c}_{t}", f"sq_{c}_{t}"], tag=f"xntd_{c}_{t}")
                for dt_ in range(DT):
                    pT = p512c if dt_ % 2 == 0 else p512d
                    tw = [f"xntd_{c}_3", CW]
                    if dt_ >= 2:
                        tw += [f"xnf_{c}_{dt_-2}", f"xnb_{c}_{dt_-2}"]
                    elif c >= 1:
                        tw += [f"sgi_{c-1}_{6 + dt_}"]

                    def bxnT(e, pT=pT, dt_=dt_):
                        last = None
                        for t in range(4):
                            last = e.transpose(pT[:, t * 128:(t + 1) * 128],
                                               xn_td[:, t, dt_ * 128:(dt_ + 1) * 128],
                                               ident[:])
                        return last
                    P.emit("tensor", bxnT, waits=tw, tag=f"xnT_{c}_{dt_}")
                    bw = [f"xnT_{c}_{dt_}"]
                    if c >= 1:
                        bw += [f"ich_{c-1}_7"]
                    P.emit("vector", lambda e, pT=pT, dt_=dt_: e.tensor_copy(
                        xnbf_c[:, dt_, :], pT[:]), waits=bw, tag=f"xnb_{c}_{dt_}")
                    fw = [f"xnT_{c}_{dt_}"]
                    if c >= 1:
                        fw += [f"ixn_{c-1}_{dt_}"]
                        if c - 1 >= NP:
                            fw += [f"rtch_{c-1}"]
                    P.emit("vector", lambda e, pT=pT, dt_=dt_: e.tensor_copy(
                        xnf32_c[:, dt_, :], pT[:]), waits=fw, tag=f"xnf_{c}_{dt_}")
                if own:
                    P.emit("gpsimd", lambda e, coff=coff: e.dma_start(
                        xn_tok_d[coff:coff + CH]
                        .rearrange("(t q) d -> q t d", q=128),
                        xn_td[:]),
                        waits=[f"xntd_{c}_{t_}" for t_ in range(4)]
                        + [("semxn", P.cnt["semxn"])],
                        inc=("semxn", 16), tag=f"xnsp_{c}")

            def emit_mid(c):
                own = c >= NP
                for et in range(DT):
                    pr = p512a if et % 2 == 0 else p512b
                    rw = [f"xnb_{c}_7", WRW]
                    rn = c * 8 + et
                    if rn >= 2:
                        pc, pe = divmod(rn - 2, 8)
                        rw += [f"sgr_{pc}_{pe}"]
                    P.emit("tensor", mm_chain(
                        pr[:],
                        [wrT_sb[:, d_, et * 128:(et + 1) * 128] for d_ in range(DT)],
                        [xnbf_c[:, d_, :] for d_ in range(DT)]),
                        waits=rw, tag=f"rch_{c}_{et}")
                    sw = [f"rch_{c}_{et}"]
                    if c >= 1:
                        sw += [f"b_{c-1}_{et}"]
                    P.emit("scalar", lambda e, pr=pr, et=et: e.activation(
                        bT[:, et, :], pr[:], AF.Sigmoid),
                        waits=sw, tag=f"sgr_{c}_{et}")
                for et in range(DT):
                    pi = p512c if et % 2 == 0 else p512d
                    iw = [f"xnb_{c}_7"]
                    if et >= 2:
                        iw += [f"sgi_{c}_{et-2}"]
                    else:
                        iw += [f"xnf_{c}_{6 + et}"]
                    P.emit("tensor", mm_chain(
                        pi[:],
                        [wiT_sb[:, d_, et * 128:(et + 1) * 128] for d_ in range(DT)],
                        [xnbf_c[:, d_, :] for d_ in range(DT)]),
                        waits=iw, tag=f"ich_{c}_{et}")
                    sw = [f"ich_{c}_{et}"]
                    if c >= 1:
                        sw += [f"scan_{c-1}_{et}"]
                    P.emit("scalar", lambda e, pi=pi, et=et: e.activation(
                        bU[:, et, :], pi[:], AF.Sigmoid),
                        waits=sw, tag=f"sgi_{c}_{et}")
                for et in range(DT):
                    P.emit("vector", lambda e, et=et: e.tensor_mul(
                        bU[:, et, :], bU[:, et, :], xnf32_c[:, et, :]),
                        waits=[f"sgi_{c}_{et}", f"xnf_{c}_{et}"], tag=f"ixn_{c}_{et}")
                if own:
                    rw = [f"xnf_{c}_7", CW]
                    if c - 1 >= NP:
                        rw += [f"lgcp_{c-1}"]
                    P.emit("tensor", mm_chain(
                        p1024b[0:NE, 0:512],
                        [wrout_sb[:, d_, :] for d_ in range(DT)],
                        [xnf32_c[:, d_, :] for d_ in range(DT)]),
                        waits=rw, tag=f"rtch_{c}")

            def emit_tail_sv(c):
                for et in range(DT):
                    aw = [f"sgr_{c}_{et}", f"scl_{et}"]
                    if c >= 1:
                        aw += [f"scan_{c-1}_{et}"]
                    P.emit("scalar", lambda e, et=et, c=c: e.activation(
                        bA[:, et, :], bT[:, et, :], AF.Exp,
                        scale=scales[:, et, c:c + 1]), waits=aw,
                        tag=f"a_{c}_{et}")
                for et in range(DT):
                    P.emit("scalar", lambda e, et=et: e.activation(
                        bT[:, et, :], bA[:, et, :], AF.Square),
                        waits=[f"a_{c}_{et}"], tag=f"e2_{c}_{et}")
                for et in range(DT):
                    P.emit("scalar", lambda e, et=et: e.activation(
                        bT[:, et, :], bT[:, et, :], AF.Relu, scale=-1.0, bias=1.0),
                        waits=[f"e2_{c}_{et}"], tag=f"qm_{c}_{et}")
                for et in range(DT):
                    P.emit("scalar", lambda e, et=et: e.activation(
                        bT[:, et, :], bT[:, et, :], AF.Sqrt),
                        waits=[f"qm_{c}_{et}"], tag=f"root_{c}_{et}")
                for et in range(DT):
                    P.emit("vector", lambda e, et=et: e.tensor_mul(
                        bU[:, et, :], bT[:, et, :], bU[:, et, :]),
                        waits=[f"root_{c}_{et}", f"ixn_{c}_{et}"], tag=f"b_{c}_{et}")
                for et in range(DT):
                    sw = [f"a_{c}_{et}", f"b_{c}_{et}"]
                    if c >= 1:
                        sw += [f"cc_{c-1}_{et}"]
                        if c - 1 >= NP:
                            sw += [f"hT_{c-1}_3"]
                    init = 0.0 if c == 0 else carry[:, et:et + 1]
                    P.emit("vector", lambda e, et=et, init=init: e.tensor_tensor_scan(
                        h_sb[:, et, :], bA[:, et, :], bU[:, et, :], init,
                        ALU.mult, ALU.add), waits=sw, tag=f"scan_{c}_{et}")
                    P.emit("vector", lambda e, et=et: e.tensor_copy(
                        carry[:, et:et + 1], h_sb[:, et, CH - 1:CH]),
                        waits=[f"scan_{c}_{et}"], tag=f"cc_{c}_{et}")

            def emit_tail_pe(c):
                cb = c % 2
                XW = P.tags[f"xl_{c}"]
                for t in range(4):
                    tw = [f"scan_{c}_7", CW]
                    if t >= 1:
                        tw += [f"acc_{c}_{t-1}"]
                    elif c - 1 >= NP:
                        tw += [f"acc_{c-1}_3"]

                    def bhT(e, t=t):
                        last = None
                        for et in range(DT):
                            last = e.transpose(p1024a[:, et * 128:(et + 1) * 128],
                                               h_sb[:, et, t * 128:(t + 1) * 128],
                                               ident[:])
                        return last
                    P.emit("tensor", bhT, waits=tw, tag=f"hT_{c}_{t}")
                    sa = "semacc0" if t % 2 == 0 else "semacc1"
                    aw = [f"hT_{c}_{t}", XW]
                    P.emit("vector", lambda e, cb=cb, t=t: e.tensor_add(
                        x_sb[:, cb, t, :], x_sb[:, cb, t, :], p1024a[:]),
                        waits=aw, tag=f"acc_{c}_{t}")
                    P.emit("sync", lambda e, c=c, t=t, cb=cb: e.dma_start(
                        acc_d[(c - NP) * 4 + t], x_sb[:, cb, t, :]),
                        waits=[f"acc_{c}_{t}", (sa, P.cnt[sa])],
                        inc=(sa, 16))

            def emit_route(c):
                oc = c - NP
                for j in range(4):
                    tt = 4 * oc + j
                    P.emit("vector", lambda e, tt=tt: e.reduce_max(
                        rstat[:, tt, 0:1], lg_t[:, tt, :], axis=AX),
                        waits=[f"lgcp_{c}"], tag=f"mx_{tt}")
                    P.emit("vector", lambda e, tt=tt: e.tensor_scalar_mul(
                        rstat[:, tt, 1:2], rstat[:, tt, 0:1], -1.0),
                        waits=[f"mx_{tt}"], tag=f"nmx_{tt}")
                    P.emit("scalar", lambda e, tt=tt: e.activation(
                        ex_sc[:], lg_t[:, tt, :], AF.Exp, bias=rstat[:, tt, 1:2]),
                        waits=[f"nmx_{tt}"] + ([f"ex_{tt-1}"] if tt else []),
                        tag=f"ex_{tt}")
                    P.emit("vector", lambda e, tt=tt: e.reduce_sum(
                        rstat[:, tt, 2:3], ex_sc[:], axis=AX),
                        waits=[f"ex_{tt}"], tag=f"sm_{tt}")
                    P.emit("vector", lambda e, tt=tt: e.reciprocal(
                        rstat[:, tt, 3:4], rstat[:, tt, 2:3]),
                        waits=[f"sm_{tt}"], tag=f"gt_{tt}")
                    P.emit("vector", lambda e, tt=tt, j=j: e.tensor_scalar(
                        msk_tok[:, j, :], lg_t[:, tt, :], rstat[:, tt, 0:1],
                        None, ALU.is_equal, ALU.bypass),
                        waits=[f"gt_{tt}"], tag=f"msk_{tt}")
                    P.emit("vector", lambda e, tt=tt, j=j: e.tensor_scalar(
                        s_all[:, tt, :], msk_tok[:, j, :], rstat[:, tt, 3:4],
                        None, ALU.mult, ALU.bypass),
                        waits=[f"msk_{tt}"], tag=f"sall_{tt}")
                tw = [f"sall_{4 * oc + j}" for j in range(4)] + [CW]
                if oc >= 1:
                    tw += [f"poscp_{oc-1}"]

                def bsT(e, oc=oc):
                    last = None
                    for j in range(4):
                        last = e.transpose(
                            p1024b[0:NE, 512 + j * 128:512 + (j + 1) * 128],
                            s_all[:, 4 * oc + j, :], ident[:])
                    return last
                P.emit("tensor", bsT, waits=tw, tag=f"sT_{oc}")
                P.emit("vector", lambda e, oc=oc: e.tensor_copy(
                    sTpk[:, oc * 512:(oc + 1) * 512], p1024b[0:NE, 512:1024]),
                    waits=[f"sT_{oc}"], tag=f"sThc_{oc}")
                P.emit("vector", lambda e, oc=oc: e.tensor_tensor(
                    sterr2[:], p1024b[0:NE, 512:1024],
                    sTpk[:, oc * 512:(oc + 1) * 512], ALU.subtract),
                    waits=[f"sThc_{oc}"], tag=f"sTc_{oc}")
                P.emit("vector", lambda e, oc=oc: e.tensor_copy(
                    sTpk[:, NTT * 128 + oc * 512:NTT * 128 + (oc + 1) * 512],
                    sterr2[:]),
                    waits=[f"sTc_{oc}"], tag=f"sTlc_{oc}")
                P.emit("vector", lambda e: e.tensor_scalar(
                    mT_b[:, 0:512], p1024b[0:NE, 512:1024], 0.0, None,
                    ALU.is_gt, ALU.bypass),
                    waits=[f"sTlc_{oc}", f"lgtr_{c}"], tag=f"mT_{oc}")
                init = 0.0 if oc == 0 else cnt8[:, 0:1]
                P.emit("vector", lambda e, init=init: e.tensor_tensor_scan(
                    posT_b[:], ones8[:], mT_b[:, 0:512], init, ALU.mult, ALU.add),
                    waits=[f"mT_{oc}", "on8"]
                    + ([f"cnt8_{oc-1}"] if oc else []), tag=f"posT_{oc}")
                P.emit("vector", lambda e: e.tensor_copy(
                    cnt8[:], posT_b[:, 511:512]),
                    waits=[f"posT_{oc}"], tag=f"cnt8_{oc}")

                def bpT(e):
                    last = None
                    for j in range(4):
                        last = e.transpose(
                            p1024b[:, 512 + j * NE:512 + (j + 1) * NE],
                            posT_b[:, j * 128:(j + 1) * 128],
                            ident[0:NE, 0:NE])
                    return last
                P.emit("tensor", bpT, waits=[f"posT_{oc}", f"mT_{oc}"],
                       tag=f"posbT_{oc}")
                P.emit("vector", lambda e: e.tensor_copy(
                    pos_tok[:].rearrange("p t e -> p (t e)"),
                    p1024b[:, 512:512 + 4 * NE]),
                    waits=[f"posbT_{oc}"], tag=f"poscp_{oc}")
                for j in range(4):
                    tt = 4 * oc + j
                    P.emit("vector", lambda e, j=j: e.tensor_scalar(
                        dtmp[:], pos_tok[:, j, :], float(C_CAP), None,
                        ALU.min, ALU.bypass),
                        waits=[f"poscp_{oc}", f"dg_{tt-1}" if tt else None],
                        tag=f"da_{tt}")
                    P.emit("vector", lambda e: e.tensor_add(
                        dtmp[:], dtmp[:], eCm1[:]),
                        waits=[f"da_{tt}", ECW], tag=f"db_{tt}")
                    P.emit("vector", lambda e, j=j: e.tensor_mul(
                        dtmp[:], dtmp[:], msk_tok[:, j, :]),
                        waits=[f"db_{tt}"], tag=f"dc_{tt}")
                    P.emit("vector", lambda e, j=j: e.reduce_sum(
                        dest_f4[:, j:j + 1], dtmp[:], axis=AX),
                        waits=[f"dc_{tt}"], tag=f"dg_{tt}")
                P.emit("vector", lambda e, oc=oc: e.tensor_copy(
                    dest_i[:, oc * 4:(oc + 1) * 4], dest_f4[:]),
                    waits=[f"dg_{4 * oc + 3}"], tag=f"dsti_{oc}")
                xnspW = P.tags[f"xnsp_{c}"]
                for hb in range(2):
                    scprev = ("semsc", P.cnt["semsc"])
                    P.emit("sync", lambda e, oc=oc, hb=hb: e.dma_start(
                        xbnc[:], xn_tok_d[oc * 512 + hb * 256:
                                          oc * 512 + (hb + 1) * 256]
                        .rearrange("(t q2) d -> q2 t d", q2=128)),
                        waits=[xnspW, scprev, ("semt0", P.cnt["semt0"])]
                        + ([ZFW] if oc == 0 and hb == 0 else []),
                        inc=("semt0", 16))
                    XBW = ("semt0", P.cnt["semt0"])
                    for j2 in range(2):
                        tt = 4 * oc + hb * 2 + j2
                        P.emit("gpsimd", lambda e, tt=tt, j2=j2:
                               e.indirect_dma_start(
                                   xsort_d,
                                   bass.IndirectOffsetOnAxis(
                                       dest_i[:, tt:tt + 1], 0),
                                   xbnc[:, j2], None),
                               waits=[ZFW, XBW, f"dsti_{oc}"],
                               inc=("semsc", 16))

            def emit_lg(c):
                coff = (c - NP) * CH
                rtw = [f"rtch_{c}", CW]
                if c - 1 >= NP:
                    rtw += [f"lgtr_{c-1}", f"posT_{c-1-NP}"]
                P.emit("vector", lambda e: e.tensor_scalar(
                    lgev[:], p1024b[0:NE, 0:512], br_sb[:, 0:1], None,
                    ALU.add, ALU.bypass),
                    waits=rtw, tag=f"rt_{c}")

                def blgtr(e):
                    last = None
                    for t in range(4):
                        last = e.transpose(p1024b[:, t * NE:(t + 1) * NE],
                                           lgev[:, t * 128:(t + 1) * 128],
                                           ident[0:NE, 0:NE])
                    return last
                P.emit("tensor", blgtr, waits=[f"rt_{c}"], tag=f"lgtr_{c}")
                P.emit("vector", lambda e, coff=coff: e.tensor_copy(
                    lg_t[:, coff // 128:coff // 128 + 4, :]
                    .rearrange("p t e -> p (t e)"),
                    p1024b[:, 0:4 * NE]),
                    waits=[f"lgtr_{c}"], tag=f"lgcp_{c}")

            for c in range(NCH + 1):
                if c < NCH:
                    emit_head(c)
                if c >= 1:
                    emit_tail_sv(c - 1)
                    if c - 1 >= NP:
                        emit_tail_pe(c - 1)
                        emit_lg(c - 1)
                        emit_route(c - 1)
                if c < NCH:
                    emit_mid(c)

            P.tags["desti"] = P.tags[f"dsti_{NO - 1}"]
            ACCW0 = ("semacc0", P.cnt["semacc0"])
            ACCW1 = ("semacc1", P.cnt["semacc1"])
            XNW = ("semxn", P.cnt["semxn"])
            endA = [(s, P.cnt[s]) for s in
                    ["sTE", "sDVE", "sACT", "semxn", "semacc0", "semacc1",
                     "semx0", "semx1", "semwi"]]

            with nc.Block() as blkA:
                P.flush(blkA)

        with ExitStack() as phB:
            def bsb(n_, s, dt=F32):
                return phB.enter_context(nc.sbuf_tensor(n_, s, dt))
            sterr = bsb("sterr", [NE, 512])
            ygt = bsb("ygt", [128, 4, 1024])
            yacc = bsb("yacc", [128, 16, 1024])
            w1c = bsb("w1c", [128, 2, DT, 512], BF16)
            w2c = bsb("w2c", [128, 2, 4, 1024], BF16)
            h1m = bsb("h1m", [128, 2, 4, CH], BF16)
            xse = bsb("xse", [128, 2, DT, C_CAP], BF16)

            for j in range(2):
                P.emit("vector", lambda e, j=j: e.tensor_sub(
                    sterr[:], b2_sb[:, j * 512:(j + 1) * 512],
                    b2pk[:, j * 512:(j + 1) * 512]),
                    waits=endA + [CW, B2W] + ([f"b2l_{j-1}"] if j else []),
                    tag=f"b2e_{j}")
                P.emit("vector", lambda e, j=j: e.tensor_copy(
                    b2pk[:, D + j * 512:D + (j + 1) * 512], sterr[:]),
                    waits=[f"b2e_{j}"], tag=f"b2l_{j}")


            SCW = ("semsc", P.cnt["semsc"])

            fi_n = 0
            g_n = 0
            tt_n = 0
            ytag = []
            ylast = {}
            wdone = []
            yprev = {}
            hc_by_exp = []
            yspart = {}
            for q in range(NQ):
                for e2 in range(EPQ):
                    e_ = q * EPQ + e2
                    xpar = e_ % 2
                    sxs = "semxs0" if xpar == 0 else "semxs1"
                    xw2 = [SCW, (sxs, P.cnt[sxs])]
                    if e_ >= 2:
                        xw2 += [hc_by_exp[e_ - 2]]
                    else:
                        xw2 += endA
                    P.emit("sync", lambda e, e_=e_, xpar=xpar: e.dma_start_transpose(
                        xse[:, xpar], xsort_d[e_ * C_CAP:(e_ + 1) * C_CAP]),
                        waits=xw2, inc=(sxs, 16))
                    XSW = (sxs, P.cnt[sxs])
                    for fc in range(NFC2):
                        k = e_ * NFC2 + fc
                        kb = k % 2
                        swp = "semw0" if kb == 0 else "semw1"
                        ww = [wdone[k - 2]] if k >= 2 else list(endA)
                        ww += [(swp, P.cnt[swp])]
                        P.emit("gpsimd", lambda e, e_=e_, fc=fc, kb=kb: e.dma_start(
                            w1c[:, kb], w1[e_, :, fc * 512:(fc + 1) * 512]
                            .rearrange("(dt q2) f -> q2 dt f", q2=128)),
                            waits=ww, inc=(swp, 16))
                        P.emit("gpsimd", lambda e, e_=e_, fc=fc, kb=kb: e.dma_start(
                            w2c[:, kb], w2[e_, fc * 512:(fc + 1) * 512, :]
                            .rearrange("(ft q2) n -> q2 ft n", q2=128)),
                            inc=(swp, 16))
                        wlev = (swp, P.cnt[swp])

                        def emit_Hpair():
                            nonlocal fi_n
                            for fi in range(4):
                                pA, pB = ((p512a, p512b) if fi_n % 2 == 0
                                          else (p512c, p512d))
                                hw = [wlev, XSW]
                                if fi_n >= 2:
                                    hw += [f"heB_{fi_n - 2}"]

                                def bH(e, pA=pA, pB=pB, fi=fi, kb=kb, xpar=xpar):
                                    last = None
                                    for d_ in range(DT):
                                        w = w1c[:, kb, d_, fi * 128:(fi + 1) * 128]
                                        last = e.matmul(
                                            pA[:], w, xse[:, xpar, d_, 0:CH],
                                            start=(d_ == 0), stop=(d_ == DT - 1))
                                        last = e.matmul(
                                            pB[:], w, xse[:, xpar, d_, CH:2 * CH],
                                            start=(d_ == 0), stop=(d_ == DT - 1))
                                    return last
                                P.emit("tensor", bH, waits=hw, tag=f"hc_{fi_n}")
                                col = e_ * NFC * 8 + fc * 4 + fi
                                swA = [f"hc_{fi_n}"]
                                swB = [f"hc_{fi_n}"]
                                if 0 in ylast:
                                    swA += [ylast[0]]
                                if 1 in ylast:
                                    swB += [ylast[1]]
                                P.emit("scalar", lambda e, pA=pA, fi=fi, col=col:
                                       e.activation(h1m[:, 0, fi, :], pA[:],
                                                    AF.Relu,
                                                    bias=b1_sb[:, col:col + 1]),
                                       waits=swA, tag=f"heA_{fi_n}")
                                P.emit("scalar", lambda e, pB=pB, fi=fi, col=col:
                                       e.activation(h1m[:, 1, fi, :], pB[:],
                                                    AF.Relu,
                                                    bias=b1_sb[:, col:col + 1]),
                                       waits=swB, tag=f"heB_{fi_n}")
                                fi_n += 1

                        def emit_Y(g):
                            nonlocal tt_n
                            helast = (f"heA_{fi_n - 1}" if g == 0
                                      else f"heB_{fi_n - 1}")
                            lastyc = None
                            for tti in range(4):
                                tile = e2 * TPE + g * 4 + tti
                                py = p1024a if tt_n % 2 == 0 else p1024b
                                yw2 = [helast]
                                if tt_n >= 2:
                                    yw2 += [ytag[tt_n - 2]]

                                def bY(e, py=py, g=g, tti=tti, kb=kb):
                                    last = None
                                    for fi in range(4):
                                        for dh in range(2):
                                            last = e.matmul(
                                                py[:, dh * 512:(dh + 1) * 512],
                                                h1m[:, g, fi,
                                                    tti * 128:(tti + 1) * 128],
                                                w2c[:, kb, fi,
                                                    dh * 512:(dh + 1) * 512],
                                                start=(fi == 0), stop=(fi == 3))
                                    return last
                                P.emit("tensor", bY, waits=yw2, tag=f"yc_{tt_n}_1")
                                vw = [f"yc_{tt_n}_1"]
                                if fc == 0 and tile in yspart:
                                    vw += [yspart[tile]]
                                if tile in yprev:
                                    vw += [yprev[tile]]
                                if fc == 0:
                                    P.emit("vector", lambda e, py=py, tile=tile:
                                           e.tensor_copy(yacc[:, tile, :], py[:]),
                                           waits=vw, tag=f"ye_{tt_n}")
                                else:
                                    P.emit("vector", lambda e, py=py, tile=tile:
                                           e.tensor_add(yacc[:, tile, :],
                                                        yacc[:, tile, :], py[:]),
                                           waits=vw, tag=f"ye_{tt_n}")
                                ytag.append(f"ye_{tt_n}")
                                yprev[tile] = f"ye_{tt_n}"
                                lastyc = f"yc_{tt_n}_1"
                                tt_n += 1
                            ylast[g] = lastyc

                        assert NGE == 2
                        emit_Hpair()
                        emit_Y(0)
                        emit_Y(1)
                        wdone.append(f"yc_{tt_n - 1}_1")
                    hc_by_exp.append(f"hc_{fi_n - 1}")
                for part in range(4):
                    sp = ["semys", "semt1", "semx0", "semx1"][part]
                    P.emit("scalar", lambda e, q=q, part=part: e.dma_start(
                        ysort_d[q * 2048 + part * 512:q * 2048 + (part + 1) * 512]
                        .rearrange("(t q2) d -> q2 t d", q2=128),
                        yacc[:, part * 4:(part + 1) * 4, :]),
                        waits=[yprev[part * 4 + j] for j in range(4)]
                        + [(sp, P.cnt[sp])],
                        inc=(sp, 16))
                    for j in range(4):
                        yspart[part * 4 + j] = (sp, P.cnt[sp])
                yprev = {}
            YSF4 = [(sp, P.cnt[sp])
                    for sp in ["semys", "semt1", "semx0", "semx1"]]

            for p in range(2):
                ypart = {}
                for part in range(NTTH // 4):
                    yw = [ACCW0, ACCW1, ("semY", P.cnt["semY"])] + YSF4 + endA
                    if p == 1:
                        yw += [("semout", P.cnt["semout"])]
                    P.emit("sync", lambda e, p=p, part=part: e.dma_start(
                        yacc[:, part * 4:(part + 1) * 4, :],
                        acc_d[p * NTTH + part * 4:p * NTTH + (part + 1) * 4]
                        .rearrange("t q2 d -> q2 t d")),
                        waits=yw, inc=("semY", 16))
                    for j in range(4):
                        ypart[part * 4 + j] = ("semY", P.cnt["semY"])
                outparts = []
                for tt in range(NTTH):
                    ttg = p * NTTH + tt
                    par = tt % 4
                    gw = [("semgt", P.cnt["semgt"]), "desti"] + YSF4
                    if ttg >= 4:
                        gw += [f"yga_{ttg - 4}"]
                    P.emit("gpsimd", lambda e, ttg=ttg, par=par: e.indirect_dma_start(
                        ygt[:, par], None,
                        ysort_d, bass.IndirectOffsetOnAxis(
                            dest_i[:, ttg:ttg + 1], 0)),
                        waits=gw, inc=("semgt", 16))
                    GW = ("semgt", P.cnt["semgt"])
                    py = p1024a if tt_n % 2 == 0 else p1024b
                    for dh in range(2):
                        bw = [f"sTlc_{NTT // 4 - 1}", "b2l_1"]
                        if dh == 0 and tt_n >= 2:
                            bw += [ytag[tt_n - 2]]
                        P.emit("tensor", mm_chain(
                            py[:, dh * 512:(dh + 1) * 512],
                            [sTpk[:, ttg * 128:(ttg + 1) * 128],
                             sTpk[:, ttg * 128:(ttg + 1) * 128],
                             sTpk[:, NTT * 128 + ttg * 128:
                                  NTT * 128 + (ttg + 1) * 128],
                             sTpk[:, NTT * 128 + ttg * 128:
                                  NTT * 128 + (ttg + 1) * 128]],
                            [b2pk[:, dh * 512:(dh + 1) * 512],
                             b2pk[:, D + dh * 512:D + (dh + 1) * 512],
                             b2pk[:, dh * 512:(dh + 1) * 512],
                             b2pk[:, D + dh * 512:D + (dh + 1) * 512]]),
                            waits=bw, tag=f"b2c_{tt_n}_{dh}")
                    vw = [GW, ypart[tt], f"gt_{ttg}"]
                    P.emit("vector", lambda e, tt=tt, ttg=ttg, par=par:
                           e.scalar_tensor_tensor(
                               yacc[:, tt, :], ygt[:, par],
                               rstat[:, ttg, 3:4], yacc[:, tt, :],
                               ALU.mult, ALU.add),
                           waits=vw, tag=f"yga_{ttg}")
                    P.emit("vector", lambda e, py=py, tt=tt: e.tensor_add(
                        yacc[:, tt, :], yacc[:, tt, :], py[:]),
                        waits=[f"b2c_{tt_n}_0", f"b2c_{tt_n}_1", f"yga_{ttg}"],
                        tag=f"b2a_{tt_n}")
                    ytag.append(f"b2a_{tt_n}")
                    lastb2 = f"b2a_{tt_n}"
                    tt_n += 1
                    if tt % 4 == 3:
                        P.emit("scalar", lambda e, p=p, tt=tt: e.dma_start(
                            out_d[p * HALF_T + (tt - 3) * 128:
                                  p * HALF_T + (tt + 1) * 128].rearrange(
                                "(t q2) d -> q2 t d", q2=128),
                            yacc[:, tt - 3:tt + 1, :]),
                            waits=[f"b2a_{tt_n - 1 - j}" for j in range(4)]
                            + [("semout", P.cnt["semout"])],
                            inc=("semout", 16))

            P.emit("sync", lambda e: e.nop(),
                   waits=[("semout", P.cnt["semout"])])

            with nc.Block() as blkB:
                P.flush(blkB)

    return nc


def kernel(L0_attn_x, L0_attn_W_r, L0_attn_W_i, L0_attn_lambd,
           L0_ffn_Wr, L0_ffn_br, L0_ffn_W1, L0_ffn_b1, L0_ffn_W2, L0_ffn_b2):
    x = np.ascontiguousarray(np.asarray(L0_attn_x, dtype=np.float32))
    assert x.shape == (4, 8192, 1024)
    wrT = np.ascontiguousarray(np.asarray(L0_attn_W_r, np.float32).T)
    wiT = np.ascontiguousarray(np.asarray(L0_attn_W_i, np.float32).T)
    lam8 = np.ascontiguousarray(np.asarray(L0_attn_lambd, np.float32).reshape(8, 128).T)
    wrout = np.ascontiguousarray(np.asarray(L0_ffn_Wr, np.float32))
    br = np.ascontiguousarray(np.asarray(L0_ffn_br, np.float32).reshape(8, 1))
    w1 = np.ascontiguousarray(np.asarray(L0_ffn_W1, np.float32))
    b1r = np.ascontiguousarray(
        np.asarray(L0_ffn_b1, np.float32).reshape(8, 32, 128).transpose(2, 0, 1)
        .reshape(128, 256))
    w2 = np.ascontiguousarray(np.asarray(L0_ffn_W2, np.float32))
    b2 = np.ascontiguousarray(np.asarray(L0_ffn_b2, np.float32))
    ident = np.eye(128, dtype=np.float32)
    ecm1 = np.ascontiguousarray(np.broadcast_to(
        np.arange(8, dtype=np.float32) * 1024.0 - 1.0, (128, 8)).astype(np.float32))

    nc = build_nc()
    in_maps = []
    for core in range(8):
        b, hh = core // 2, core % 2
        xo = np.ascontiguousarray(x[b, hh * 4096:(hh + 1) * 4096])
        xp = np.ascontiguousarray(x[b, 0:4096]) if hh == 1 else xo
        maskv = np.ones((128, 16), dtype=np.float32)
        if hh == 0:
            maskv[:, 0:8] = 0.0
        in_maps.append(dict(
            xprev=xp, xown=xo, wrT=wrT, wiT=wiT, lam8=lam8, maskv=maskv,
            wrout=wrout, br=br, w1=w1, b1r=b1r, w2=w2, b2=b2, ident=ident,
            ecm1=ecm1))

    res = run_bass_kernel_spmd(nc, in_maps, core_ids=list(range(8)))
    out = np.empty((4, 8192, 1024), dtype=np.float32)
    for core in range(8):
        b, hh = core // 2, core % 2
        out[b, hh * 4096:(hh + 1) * 4096] = res.results[core]["out"]
    return out
